# version 19
# baseline (speedup 1.0000x reference)
import contextlib
import numpy as np

import concourse.bass as bass
import concourse.mybir as mybir
import concourse.bass_utils as _bu
from concourse.bass_utils import run_bass_kernel_spmd

_orig_run_command = _bu.run_command


def _patched_run_command(cmd, **kw):
    if isinstance(cmd, list):
        cmd = ["--enable-ldw-opt=true" if c == "--enable-ldw-opt=false" else c
               for c in cmd]
    return _orig_run_command(cmd, **kw)


_bu.run_command = _patched_run_command

F32 = mybir.dt.float32
BF16 = mybir.dt.bfloat16
AF = mybir.ActivationFunctionType
ALU = mybir.AluOpType
AX = mybir.AxisListType

ENGINES = ("sync", "scalar", "vector", "tensor", "gpsimd")
PSEM = {e: "P_" + e for e in ENGINES}
DSEM = {e: "D_" + e for e in ENGINES}

REAL_CFG = dict(DC=8, TT=512, NTA=8, TMEGA=512, NMEGA=8, FC=8)


class Plan:

    def __init__(self):
        self.q = {e: [] for e in ENGINES}
        self.cnt = {}
        self.clock = {e: {} for e in ENGINES}
        self.buf = {}

    def _wait(self, eng, sem, val):
        if val <= 0 or self.clock[eng].get(sem, 0) >= val:
            return
        self.clock[eng][sem] = val
        self.q[eng].append(("w", sem, val))

    def alias(self, new_key, phys_key):
        self.buf[new_key] = self.buf[phys_key]

    def op(self, eng, fn, reads=(), writes=(), dma=False, cc=False,
           dsem=None, dma_amt=16):
        if cc:
            sem, amt = "CC", None
        elif dma:
            if dsem is not None:
                sem, amt = dsem, dma_amt
            else:
                sem, amt = DSEM[eng], 16
                self._wait(eng, sem, self.cnt.get(sem, 0))
        else:
            sem, amt = PSEM[eng], 1
        for k in reads:
            st = self.buf.get(k)
            if st is None:
                continue
            for s, v in st["w"].items():
                if eng == "tensor" and s == PSEM["tensor"]:
                    continue
                self._wait(eng, s, v)
        for k in writes:
            st = self.buf.get(k)
            if st is not None:
                for s, v in st["w"].items():
                    if eng == "tensor" and s == PSEM["tensor"]:
                        continue
                    self._wait(eng, s, v)
                for s, v in st["r"].items():
                    if eng == "tensor" and s == PSEM["tensor"]:
                        continue
                    self._wait(eng, s, v)
        newv = self.cnt.get(sem, 0) + (1 if cc else (amt or 1))
        self.cnt[sem] = newv
        self.q[eng].append(("o", fn, sem,
                            (None if cc else (16 if dma else amt))))
        for k in writes:
            self.buf[k] = {"w": {sem: newv}, "r": {}}
        for k in reads:
            st = self.buf.get(k)
            if st is None:
                st = self.buf[k] = {"w": {}, "r": {}}
            st["r"][sem] = max(st["r"].get(sem, 0), newv)
        return newv

    def raw_wait(self, eng, sem, val):
        self._wait(eng, sem, val)

    def run(self, eng, handle, sems):
        for item in self.q[eng]:
            if item[0] == "w":
                handle.wait_ge(sems[item[1]], item[2])
            else:
                _, fn, sem, amt = item
                ins = fn(handle)
                if amt is None:
                    ins.then_inc(sems[sem])
                else:
                    ins.then_inc(sems[sem], amt)


def build_nc(cfg):
    DC = cfg["DC"]
    D = DC * 128
    NC = DC
    TT = cfg["TT"]
    CPT = TT // 128
    NTA = cfg["NTA"]
    TOK = TT * NTA
    TMEGA = cfg["TMEGA"]
    NTT = TMEGA // TT
    NTC = TMEGA // 128
    NMEGA = cfg["NMEGA"]
    FC = cfg["FC"]
    NE = 8
    DH = D // 2
    NDH = DH // 128
    THW = min(TMEGA, 512)
    NTH = TMEGA // THW
    EPS = 1e-5
    assert TT <= 512 and DH <= 512 and THW <= 512 and TOK == NMEGA * TMEGA

    nc = bass.Bass("TRN2", target_bir_lowering=False)

    def I(name, shape, dt=F32):
        return nc.dram_tensor(name, list(shape), dt, kind="ExternalInput").ap()

    x4 = I("x4", (TOK, D))
    wa_t = I("wa_t", (D, D))
    wb_t = I("wb_t", (D, D))
    wc_t = I("wc_t", (D, D))
    gw_in = I("gw_in", (128, DC * 8))
    gb8_in = I("gb8_in", (1, 8 * NTC))
    w1_all = I("w1_all", ((NE + 1) * 2 * FC * 128, D))
    w2_all = I("w2_all", ((NE + 1) * FC * 128, D))
    b1s_in = I("b1s_in", (128, (NE + 1) * 2 * FC))
    rb2_in = I("rb2_in", (NE, D))
    wcb_in = I("wcb_in", (1, D))
    sb2_in = I("sb2_in", (1, D))
    alb_in = I("alb_in", (128, NC))
    bbb_in = I("bbb_in", (128, NC))
    lnw_in = I("lnw_in", (128, DC))
    lnb_in = I("lnb_in", (128, DC))
    dpar_in = I("dpar_in", (128, DC))
    m8r_in = I("m8r_in", (128, 8))
    mc8r_in = I("mc8r_in", (128, 8))
    mb8r_in = I("mb8r_in", (128, 8))
    mbc8r_in = I("mbc8r_in", (128, 8))
    mi_in = I("mi_in", (128, 16 * NC), BF16)
    mic_in = I("mic_in", (128, 16 * NC), BF16)
    mbi_in = I("mbi_in", (128, 16 * NC), BF16)
    mbic_in = I("mbic_in", (128, 16 * NC), BF16)
    id16_in = I("id16_in", (128, 128), BF16)
    id32_in = I("id32_in", (128, 128))

    y4 = nc.dram_tensor("y4", [TOK, D], F32, kind="ExternalOutput").ap()

    cc_in = nc.dram_tensor("cc_in", [2, D], F32)
    cc_out = nc.dram_tensor("cc_out", [16, D], F32)
    wgt16 = nc.dram_tensor("wgt16", [(NE + 1) * 3 * FC * 128, D], BF16)

    plan = Plan()

    with contextlib.ExitStack() as es:
        def SB(name, shape, dt):
            return es.enter_context(nc.sbuf_tensor(name, list(shape), dt))
        def PSUM(name, shape, dt):
            return es.enter_context(nc.psum_tensor(name, list(shape), dt))

        WA16 = SB("WA16", (128, DC * D), BF16)
        WB16 = SB("WB16", (128, DC * D), BF16)
        WC16 = SB("WC16", (128, NC * D), BF16)
        GW16 = SB("GW16", (128, DC * 8), BF16)
        GB8R = SB("GB8R", (1, 8 * NTC), F32)
        GBR = SB("GBR", (128, 8 * NTC), F32)
        DIAG16 = SB("DIAG16", (128, D), BF16)
        DRC = SB("DRC", (128, DC), F32)
        LNW = SB("LNW", (128, DC), F32)
        LNB = SB("LNB", (128, DC), F32)
        ALB = SB("ALB", (128, NC), F32)
        BBB = SB("BBB", (128, NC), F32)
        BBB16 = SB("BBB16", (1, D), BF16)
        B1S = SB("B1S", (128, (NE + 1) * 2 * FC), F32)
        RB2A16 = SB("RB2A16", (9, D), BF16)
        ONES16 = SB("ONES16", (1, max(TT, TMEGA)), BF16)
        ONE1 = SB("ONE1", (1, 128), F32)
        EPSC = SB("EPSC", (128, 1), F32)
        NHC = SB("NHC", (128, 1), F32)
        NE1C = SB("NE1C", (128, 1), F32)
        ID16 = SB("ID16", (128, 128), BF16)
        ID32 = SB("ID32", (128, 128), F32)
        MI = SB("MI", (128, 16 * NC), BF16)
        MIC = SB("MIC", (128, 16 * NC), BF16)
        MBI = SB("MBI", (128, 16 * NC), BF16)
        MBIC = SB("MBIC", (128, 16 * NC), BF16)
        FCAR = SB("FCAR", (128, 16 * NC), F32)
        FG = SB("FG", (128, 16 * NC), F32)
        CFO = SB("CFO", (128, 8 * NC), F32)
        GFO = [SB("GFO0", (128, 8 * NC), F32),
               SB("GFO1", (128, 8 * NC), F32)]
        ATOT = SB("ATOT", (128, NC), F32)
        HLAST = SB("HLAST", (128, NC), F32)
        NSC = TOK // 128
        S1T = SB("S1T", (128, NSC), F32)
        S2T = SB("S2T", (128, NSC), F32)
        MUT = SB("MUT", (128, NSC), F32)
        RT = SB("RT", (128, NSC), F32)
        RPT = SB("RPT", (128, NSC), F32)
        VT = SB("VT", (128, NSC), F32)
        VA = SB("VA", (128, NSC), F32)
        VB = SB("VB", (128, NSC), F32)
        ZBB = VT
        XBB = VA
        XCH = [SB(f"XCH{i}", (128, D), F32) for i in range(4)]
        ZCH = [SB("ZCH0", (128, D), BF16), SB("ZCH1", (128, D), BF16)]
        SQD = ZCH[0]
        CCB = XCH[0]
        ZT = [SB("ZT0", (128, DC * TT), BF16), SB("ZT1", (128, DC * TT), BF16)]
        ALS = [SB("ALS0", (128, TT), F32), SB("ALS1", (128, TT), F32)]
        HS = [SB("HS0", (128, TT), F32), SB("HS1", (128, TT), F32)]
        WCS = [SB("WCS0", (128, D), BF16), SB("WCS1", (128, D), BF16)]
        XNCH = WCS
        BT0 = XCH[0]
        BT1 = XCH[1]
        OUTA = SB("OUTA", (128, NTC * D), F32)
        NWST = 2 if NTC >= 2 else 1
        WST = [OUTA[:, 0:D], OUTA[:, D:2 * D] if NTC >= 2 else OUTA[:, 0:D]]
        XNT2 = [SB("XNT0", (128, DC * TMEGA), BF16),
                SB("XNT1", (128, DC * TMEGA), BF16)]
        HMEGA = SB("HMEGA", (128, NC * TMEGA), BF16)
        HLOC = SB("HLOC", (128, NC * TMEGA), BF16)
        PLOC = SB("PLOC", (128, NC * TMEGA), BF16)

        SWS = SB("SWS", (128, FC * TMEGA), BF16)
        W1P = [SB(f"W1P{i}", (128, 2 * D), BF16) for i in range(3)]
        W2E2 = [SB("W2E0", (128, FC * DH), BF16),
                SB("W2E1", (128, FC * DH), BF16)]
        SIL = [SB("SIL0", (128, THW), F32), SB("SIL1", (128, THW), F32)]
        GT0 = SB("GT0", (128, 8 * NTC), F32)
        ETB = GT0
        MX = SB("MX", (128, NTC), F32)
        NMX = SB("NMX", (128, NTC), F32)
        SMS = SB("SMS", (128, NTC), F32)
        SIV = SB("SIV", (128, NTC), F32)
        M1 = SB("M1", (128, NTC), F32)
        M2 = SB("M2", (128, NTC), F32)
        PMT = SB("PMT", (128, 8), F32)
        CWU = PMT
        CWM2 = [SB("CWM0", (128, 8 * NTC), F32),
                SB("CWM1", (128, 8 * NTC), F32)]
        CWAUG1T = SB("CWAUG", (9, TMEGA), BF16)
        CWAUG2 = [CWAUG1T, CWAUG1T]
        CWSB = SB("CWSB", (8, 128), F32)
        CWSB2 = [CWSB, CWSB]
        AJH = SB("AJH", (128, 16 * NC), F32)

        PSW = max(TMEGA, D, 512)
        PS_A = PSUM("PS_A", (128, PSW), F32)
        PS_B = PSUM("PS_B", (128, PSW), F32)
        PS_C = PSUM("PS_C", (128, PSW), F32)
        PS_T1 = PSUM("PS_T1", (128, 1024), BF16)
        PS_T2 = PSUM("PS_T2", (128, 1024), BF16)
        PCW = PSW // 2

        sem_names = ([PSEM[e] for e in ENGINES] + [DSEM[e] for e in ENGINES]
                     + ["CC"] + [f"XS{i}" for i in range(4)]
                     + ["WP0", "WP1", "WP2", "WE0", "WE1", "WS0", "WS1",
                        "TZ0", "TZ1", "TX0", "TX1"])
        sems = {n: es.enter_context(nc.semaphore(n)) for n in sem_names}

        op = plan.op

        def stats_chain(ncols):
            sl = slice(0, ncols)
            rs1 = [("S1T", c) for c in range(ncols)]
            rs2 = [("S2T", c) for c in range(ncols)]
            op("vector", lambda e: e.tensor_scalar(MUT[:, sl], S1T[:, sl],
               1.0 / D, None, ALU.mult), reads=rs1, writes=(("MUT",),))
            op("vector", lambda e: e.tensor_scalar(VA[:, sl], S2T[:, sl],
               1.0 / D, None, ALU.mult), reads=rs2, writes=(("VA",),))
            op("vector", lambda e: e.tensor_tensor(VB[:, sl], MUT[:, sl],
               MUT[:, sl], ALU.mult), reads=(("MUT",),), writes=(("VB",),))
            op("vector", lambda e: e.tensor_tensor(VT[:, sl], VA[:, sl],
               VB[:, sl], ALU.subtract),
               reads=(("VA",), ("VB",)), writes=(("VT",),))
            op("scalar", lambda e: e.activation(VA[:, sl], VT[:, sl], AF.Ln,
               bias=EPSC[:, 0:1]), reads=(("VT",), ("EPSC",)), writes=(("VA",),))
            op("scalar", lambda e: e.activation(RT[:, sl], VA[:, sl], AF.Exp,
               scale=NHC[:, 0:1]), reads=(("VA",), ("NHC",)), writes=(("RT",),))
            op("scalar", lambda e: e.activation(VB[:, sl], VA[:, sl], AF.Exp,
               scale=NE1C[:, 0:1]), reads=(("VA",), ("NE1C",)), writes=(("VB",),))
            op("vector", lambda e: e.tensor_tensor(VB[:, sl], VT[:, sl],
               VB[:, sl], ALU.mult), reads=(("VT",), ("VB",)), writes=(("VB",),))
            op("scalar", lambda e: e.activation(VB[:, sl], VB[:, sl], AF.Ln,
               bias=EPSC[:, 0:1]), reads=(("VB",), ("EPSC",)), writes=(("VB",),))
            op("scalar", lambda e: e.activation(VB[:, sl], VB[:, sl], AF.Exp,
               scale=NHC[:, 0:1]), reads=(("VB",), ("NHC",)), writes=(("VB",),))
            op("vector", lambda e: e.tensor_tensor(RPT[:, sl], RT[:, sl],
               VB[:, sl], ALU.mult), reads=(("RT",), ("VB",)), writes=(("RPT",),))
            op("vector", lambda e: e.tensor_tensor(ZBB[:, sl], MUT[:, sl],
               RPT[:, sl], ALU.mult), reads=(("MUT",), ("RPT",)),
               writes=(("VT",),))
            op("vector", lambda e: e.tensor_scalar(ZBB[:, sl], ZBB[:, sl],
               -1.0, None, ALU.mult), reads=(("ZBB",),), writes=(("VT",),))
            op("vector", lambda e: e.tensor_tensor(XBB[:, sl], MUT[:, sl],
               RT[:, sl], ALU.mult), reads=(("MUT",), ("RT",)),
               writes=(("VA",),))
            op("vector", lambda e: e.tensor_scalar(XBB[:, sl], XBB[:, sl],
               -1.0, None, ALU.mult), reads=(("XBB",),), writes=(("VA",),))

        def emit_transpose_dma(src_ap_fn, src_key_fn, dst_ap_fn, dst_key_fn,
                               n, eng, dsem):
            for i in range(n):
                op(eng, lambda e, s=src_ap_fn(i), d=dst_ap_fn(i):
                   e.dma_start_transpose(d, s),
                   reads=(src_key_fn(i),), writes=(dst_key_fn(i),),
                   dma=True, dsem=dsem, dma_amt=32)

        tr_state = {"slot": 0}
        def emit_transpose(src_ap_fn, src_key_fn, dst_ap_fn, dst_key_fn, n):
            for i in range(n):
                slot = tr_state["slot"]
                tr_state["slot"] = slot + 1
                psb = PS_T1 if slot % 2 == 0 else PS_T2
                pkey = ("PSTR", slot % 2)
                pap = psb[:, 0:128]
                op("tensor", lambda e, pap=pap, s=src_ap_fn(i):
                   e.transpose(pap, s, ID16[:, :]),
                   reads=(src_key_fn(i), ("ID16",)), writes=(pkey,))
                op("scalar", lambda e, pap=pap, d=dst_ap_fn(i):
                   e.activation(d, pap, AF.Copy),
                   reads=(pkey,), writes=(dst_key_fn(i),))

        def attn_round(k):
            zt = ZT[k % 2]
            ztile_keys = tuple(("ZTp", k % 2, dc, c)
                               for dc in range(DC) for c in range(CPT))
            for ncj in range(NC):
                pa = PS_A[:, (ncj % 2) * TT:((ncj % 2) + 1) * TT]
                pu = PS_B[:, (ncj % 2) * TT:((ncj % 2) + 1) * TT]
                ka = ("PSA", ncj % 2)
                ku = ("PSB", ncj % 2)
                def _amm(e, ncj=ncj, pa=pa, zt=zt):
                    last = None
                    for dc in range(DC):
                        last = e.matmul(pa, WA16[:, dc * D + ncj * 128:
                                                 dc * D + (ncj + 1) * 128],
                                        zt[:, dc * TT:(dc + 1) * TT],
                                        start=(dc == 0), stop=(dc == DC - 1))
                    return last
                op("tensor", _amm,
                   reads=ztile_keys + tuple(("W16", 0, dc) for dc in range(DC)),
                   writes=(ka,))
                def _umm(e, ncj=ncj, pu=pu, zt=zt):
                    for dc in range(DC):
                        e.matmul(pu, WB16[:, dc * D + ncj * 128:
                                          dc * D + (ncj + 1) * 128],
                                 zt[:, dc * TT:(dc + 1) * TT],
                                 start=(dc == 0), stop=False)
                    return e.matmul(pu, BBB16[0:1, ncj * 128:(ncj + 1) * 128],
                                    ONES16[0:1, 0:TT], start=False, stop=True)
                op("tensor", _umm,
                   reads=ztile_keys + tuple(("W16", 1, dc) for dc in range(DC))
                   + (("BBB16", ncj), ("ONES16",)),
                   writes=(ku,))
                ab = ALS[ncj % 2]
                op("scalar", lambda e, ab=ab, pa=pa, ncj=ncj: e.activation(
                    ab[:, :], pa, AF.Sigmoid, bias=ALB[:, ncj:ncj + 1]),
                   reads=(ka, ("ALB",)), writes=(("ALS", ncj % 2),))
                hb = HS[ncj % 2]
                op("vector", lambda e, hb=hb, ab=ab, pu=pu:
                   e.tensor_tensor_scan(hb[:, :], ab[:, :], pu, 0.0,
                                        ALU.mult, ALU.add),
                   reads=(("ALS", ncj % 2), ku), writes=(("HS", ncj % 2),))
                op("vector", lambda e, ncj=ncj, ab=ab:
                   e.tensor_tensor_scan(
                       PLOC[:, ncj * TMEGA:(ncj + 1) * TMEGA],
                       ab[:, :], ab[:, :], 1.0, ALU.mult, ALU.bypass),
                   reads=(("ALS", ncj % 2),), writes=(("PLOC", ncj),))
                op("vector", lambda e, hb=hb, ncj=ncj: e.tensor_copy(
                    HLOC[:, ncj * TMEGA:(ncj + 1) * TMEGA], hb[:, :]),
                   reads=(("HS", ncj % 2),), writes=(("HLOC", ncj),))
                op("vector", lambda e, hb=hb, ncj=ncj: e.tensor_copy(
                    HLAST[:, ncj:ncj + 1], hb[:, TT - 1:TT]),
                   reads=(("HS", ncj % 2),), writes=(("HLAST", ncj),))
                op("vector", lambda e, ab=ab, ncj=ncj: e.tensor_reduce(
                    ATOT[:, ncj:ncj + 1], ab[:, :], AX.X, ALU.mult),
                   reads=(("ALS", ncj % 2),), writes=(("ATOT", ncj),))
        def round_pack(k):
            for which, tile, keys in (
                    (0, ATOT, tuple(("ATOT", j) for j in range(NC))),
                    (1, HLAST, tuple(("HLAST", j) for j in range(NC)))):
                op("tensor", lambda e, t=tile: e.transpose(
                    PS_C[0:NC, 0:128], t[:, :], ID32[:, :]),
                   reads=keys + (("ID32",),), writes=(("PSC", 0),))
                op("scalar", lambda e: e.activation(SUMR[0:NC, :],
                   PS_C[0:NC, 0:128], AF.Copy),
                   reads=(("PSC", 0),), writes=(("CWSB", 0),))
                op("scalar", lambda e, w=which: e.dma_start(
                    out=cc_in[w:w + 1, :].rearrange("o (c p) -> o c p", p=128),
                    in_=SUMR[0:NC, :]),
                   reads=(("CWSB", 0),), writes=(("CCI", which),), dma=True)
            op("gpsimd", lambda e: e.collective_compute(
                "AllGather", ALU.bypass, replica_groups=[list(range(8))],
                ins=[cc_in.ap().opt()], outs=[cc_out.ap().opt()]),
               reads=(("CCI", 0), ("CCI", 1)), writes=(("CCO",),), cc=True)

        def carry_round(k):
            op("scalar", lambda e: e.dma_start(out=CCB[0:16, :],
               in_=cc_out[:, :]),
               reads=(("CCO",),), writes=(("XCH", 0),), dma=True)
            plan.alias(("CCB",), ("XCH", 0))
            for c in range(NC):
                psb = PS_A if c % 2 == 0 else PS_B
                pkk = ("PSA", 0) if c % 2 == 0 else ("PSB", 0)
                op("tensor", lambda e, c=c, psb=psb: e.transpose(
                    psb[0:128, 0:16], CCB[0:16, c * 128:(c + 1) * 128],
                    ID32[0:16, 0:16]),
                   reads=(("CCB",), ("ID32",)), writes=(pkk,))
                op("scalar", lambda e, c=c, psb=psb: e.activation(
                    AJH[:, c * 16:(c + 1) * 16], psb[0:128, 0:16],
                    AF.Copy),
                   reads=(pkk,), writes=(("AJH", c),))
            ajhk = tuple(("AJH", c) for c in range(NC))
            gs, gd = (k + 1) % 2, k % 2
            for (dst, dkey, mm, mmc) in ((FCAR, ("FCAR",), MI, MIC),
                                         (FG, ("FG",), MBI, MBIC)):
                op("vector", lambda e, dst=dst, mm=mm: e.tensor_tensor(
                    dst[:, :], AJH[:, :], mm[:, :], ALU.mult),
                   reads=ajhk + (("MI",), ("MBI",)), writes=(dkey,))
                op("vector", lambda e, dst=dst, mmc=mmc: e.tensor_tensor(
                    dst[:, :], dst[:, :], mmc[:, :], ALU.add),
                   reads=(dkey, ("MIC",), ("MBIC",)), writes=(dkey,))
            for c in range(NC):
                op("vector", lambda e, c=c, gs=gs: e.tensor_tensor_scan(
                    CFO[:, c * 8:(c + 1) * 8],
                    FCAR[:, c * 16:(c + 1) * 16].rearrange(
                        "p (j t) -> p j t", t=2)[:, :, 0],
                    FCAR[:, c * 16:(c + 1) * 16].rearrange(
                        "p (j t) -> p j t", t=2)[:, :, 1],
                    GFO[gs][:, c * 8 + 7:c * 8 + 8],
                    ALU.mult, ALU.add),
                   reads=(("FCAR",), ("GFO", gs, c)), writes=(("CFO", c),))
                op("vector", lambda e, c=c, gs=gs, gd=gd: e.tensor_tensor_scan(
                    GFO[gd][:, c * 8:(c + 1) * 8],
                    FG[:, c * 16:(c + 1) * 16].rearrange(
                        "p (j t) -> p j t", t=2)[:, :, 0],
                    FG[:, c * 16:(c + 1) * 16].rearrange(
                        "p (j t) -> p j t", t=2)[:, :, 1],
                    GFO[gs][:, c * 8 + 7:c * 8 + 8],
                    ALU.mult, ALU.add),
                   reads=(("FG",), ("GFO", gs, c)), writes=(("GFO", gd, c),))
            for ncj in range(NC):
                op("vector", lambda e, ncj=ncj: e.scalar_tensor_tensor(
                    HMEGA[:, ncj * TMEGA:(ncj + 1) * TMEGA],
                    PLOC[:, ncj * TMEGA:(ncj + 1) * TMEGA],
                    CFO[:, ncj * 8 + 7:ncj * 8 + 8],
                    HLOC[:, ncj * TMEGA:(ncj + 1) * TMEGA],
                    ALU.mult, ALU.add),
                   reads=(("PLOC", ncj), ("HLOC", ncj), ("CFO", ncj)),
                   writes=(("HM", ncj, 0),))

        def ld(dst_ap, src_ap, key, reads=()):
            op("sync", lambda e, o=dst_ap, i=src_ap: e.dma_start(out=o, in_=i),
               reads=reads, writes=(key,), dma=True)

        ld(LNW[:, :], lnw_in, ("LNW",))
        ld(LNB[:, :], lnb_in, ("LNB",))
        ld(DRC[:, :], dpar_in, ("DRC",))
        ld(MI[:, :], mi_in, ("MI",))
        ld(MIC[:, :], mic_in, ("MIC",))
        ld(MBI[:, :], mbi_in, ("MBI",))
        ld(MBIC[:, :], mbic_in, ("MBIC",))
        op("vector", lambda e: e.memset(GFO[1][:, :], 0.0),
           writes=tuple(("GFO", 1, c) for c in range(NC)))
        ld(ID16[:, :], id16_in, ("ID16",))
        ld(ID32[:, :], id32_in, ("ID32",))
        ld(B1S[:, :], b1s_in, ("B1S",))

        op("sync", lambda e: e.dma_start(out=OUTA[0:1, 0:D], in_=wcb_in),
           writes=(("OUTA", 0),), dma=True)
        plan.alias(("BT0",), ("OUTA", 0))

        ld(GB8R[:, :], gb8_in, ("GB8R",))
        ld(ALB[:, :], alb_in, ("ALB",))
        ld(BBB[:, :], bbb_in, ("BBB",))
        op("vector", lambda e: e.memset(ONES16[:, :], 1.0), writes=(("ONES16",),))
        op("vector", lambda e: e.memset(ONE1[:, :], 1.0), writes=(("ONE1",),))
        op("vector", lambda e: e.memset(EPSC[:, :], EPS), writes=(("EPSC",),))
        op("vector", lambda e: e.memset(NHC[:, :], -0.5), writes=(("NHC",),))
        op("vector", lambda e: e.memset(NE1C[:, :], -1.0), writes=(("NE1C",),))
        NSC = TOK // 128
        for gi in reversed(range(NSC)):
            r0 = gi * 128
            buf = XCH[gi % 4]
            op("scalar" if gi % 2 == 0 else "sync",
               lambda e, o=buf[:, :], s=x4[r0:r0 + 128, :]:
               e.dma_start(out=o, in_=s), writes=(("XCH", gi % 4),), dma=True,
               dsem=f"XS{gi % 4}")
            plan.alias(("XPP", gi), ("XCH", gi % 4))
            op("vector", lambda e, gi=gi, xb=buf: e.reduce_sum(
                S1T[:, gi:gi + 1], xb[:, :], AX.X),
               reads=(("XPP", gi),), writes=(("S1T", gi),))
            op("scalar", lambda e, gi=gi, xb=buf: e.activation(
                SQD[:, :], xb[:, :], AF.Square,
                accum_out=S2T[:, gi:gi + 1]),
               reads=(("XPP", gi),), writes=(("S2T", gi), ("ZCH", 0)))
        stats_chain(NSC)


        op("gpsimd", lambda e: e.dma_start(out=GW16[:, :], in_=gw_in),
           writes=(("GW16",),), dma=True)
        op("tensor", lambda e: e.matmul(PS_A[:, 0:8 * NTC], ONE1[0:1, :],
                                        GB8R[0:1, :], start=True, stop=True),
           reads=(("GB8R",), ("ONE1",)), writes=(("PSA", 0),))
        op("vector", lambda e: e.tensor_copy(GBR[:, :], PS_A[:, 0:8 * NTC]),
           reads=(("PSA", 0),), writes=(("GBR",),))
        for dc in range(DC):
            op("vector", lambda e, dc=dc: e.tensor_scalar(
                DIAG16[:, dc * 128:(dc + 1) * 128], ID16[:, :],
                DRC[:, dc:dc + 1], None, ALU.mult),
               reads=(("ID16",), ("DRC",)), writes=(("DIAG16", dc),))

        op("gpsimd", lambda e: e.dma_start(out=RB2A16[0:8, :], in_=rb2_in),
           writes=(("RB2A", 0),), dma=True)
        op("gpsimd", lambda e: e.dma_start(out=OUTA[0:1, 0:D], in_=sb2_in,
                                           accum_op=ALU.add),
           reads=(("BT0",),), writes=(("OUTA", 0),), dma=True)
        plan.alias(("BT0",), ("OUTA", 0))
        plan.alias(("BT0",), ("OUTA", 0))
        op("gpsimd", lambda e: e.dma_start(out=RB2A16[8:9, :],
                                           in_=OUTA[0:1, 0:D]),
           reads=(("BT0",),), writes=(("RB2A", 8),), dma=True)
        op("gpsimd", lambda e: e.dma_start(
            out=CWAUG1T[8:9, :], in_=ONES16[0:1, 0:TMEGA]),
           reads=(("ONES16",),), writes=(("CWAUG", 8),), dma=True)
        for wsel, (src, dst16) in enumerate([(wa_t, WA16), (wb_t, WB16)]):
            for dc in range(DC):
                buf = WST[dc % NWST]
                pk = ("OUTA", dc % NWST)
                op("sync", lambda e, o=buf,
                   i=src[dc * 128:(dc + 1) * 128, :]: e.dma_start(out=o, in_=i),
                   writes=(pk,), dma=True, dsem=f"WS{dc % NWST}")
                plan.alias(("WSTP", wsel, dc), pk)
                op("vector", lambda e, o=dst16[:, dc * D:(dc + 1) * D],
                   i=buf, s=LNW[:, dc:dc + 1]:
                   e.tensor_scalar(o, i, s, None, ALU.mult),
                   reads=(("WSTP", wsel, dc), ("LNW",)),
                   writes=(("W16", wsel, dc),))
                def _mv(e, dc=dc, wsel=wsel, buf=buf):
                    last = None
                    for ncj in range(NC):
                        col = PCW + wsel * NC + ncj
                        st = (wsel == 0 and dc == 0 and ncj == 0)
                        sp = (wsel == 1 and dc == DC - 1 and ncj == NC - 1)
                        last = e.matmul(PS_C[:, col:col + 1],
                                        buf[:, ncj * 128:(ncj + 1) * 128],
                                        LNB[:, dc:dc + 1],
                                        start=st, stop=sp)
                    return last
                buf = None
                op("tensor", _mv, reads=(("WSTP", wsel, dc), ("LNB",)),
                   writes=(("PSC", 1),))
        op("vector", lambda e: e.tensor_tensor(
            ALB[:, :], ALB[:, :], PS_C[:, PCW:PCW + NC], ALU.add),
           reads=(("ALB",), ("PSC", 1)), writes=(("ALB",),))
        op("vector", lambda e: e.tensor_tensor(
            BBB[:, :], BBB[:, :], PS_C[:, PCW + NC:PCW + 2 * NC], ALU.add),
           reads=(("BBB",), ("PSC", 1)), writes=(("BBB",),))
        for ncj in range(NC):
            op("gpsimd", lambda e, ncj=ncj: e.dma_start(
                out=BBB16[0:1, ncj * 128:(ncj + 1) * 128],
                in_=BBB[:, ncj:ncj + 1]),
               reads=(("BBB",),), writes=(("BBB16", ncj),), dma=True)
        op("gpsimd", lambda e: e.dma_start(
            out=WC16[:, :].rearrange("p (n d) -> p n d", d=D),
            in_=wc_t.rearrange("(n p) d -> p n d", p=128)),
           writes=tuple(("WC16", ncj) for ncj in range(NC)), dma=True)

        SUMR = CWSB
        wjobs = []
        for e_ in list(range(1, NE + 1)) + [0]:
            wjobs.append(("w1", e_))
            wjobs.append(("w2", e_))

        def emit_phaseW():
            for kind, e_ in wjobs:
                if kind == "w1":
                    s0 = e_ * 2 * FC * 128
                    d0 = e_ * 3 * FC * 128
                    dblk = wgt16[d0:d0 + 2 * FC * 128, :].rearrange(
                        "(p q) d -> p q d", q=256)
                    for h in range(2):
                        src_ap = w1_all[s0 + h * FC * 128:
                                        s0 + (h + 1) * FC * 128, :].rearrange(
                            "(p r) d -> p r d", r=128)
                        dst_ap = dblk[:, h * 128:(h + 1) * 128, :]
                        op("gpsimd", lambda e, o=dst_ap, s=src_ap:
                           e.dma_start(out=o, in_=s),
                           writes=(("WG", e_, h),), dma=True)
                else:
                    s0 = e_ * FC * 128
                    src_ap = w2_all[s0:s0 + FC * 128, :]
                    d0 = (e_ * 3 * FC + 2 * FC) * 128
                    dst_ap = wgt16[d0:d0 + FC * 128, :]
                    op("gpsimd", lambda e, o=dst_ap, s=src_ap:
                       e.dma_start(out=o, in_=s),
                       writes=(("WG2", e_),), dma=True)

        emit_phaseW()
        gpair = {"i": 0}

        def round_S1(k):
            for c in range(CPT):
                g = k * CPT + c
                if k == 0:
                    plan.alias(("XAG", g), ("XPP", g))
                else:
                    buf = XCH[c % 4]
                    op("scalar", lambda e, o=buf[:, :],
                       s=x4[(k * TT + c * 128):(k * TT + (c + 1) * 128), :]:
                       e.dma_start(out=o, in_=s), writes=(("XCH", c % 4),),
                       dma=True, dsem=f"XS{c % 4}")
                    plan.alias(("XAG", g), ("XCH", c % 4))
            for c in range(CPT):
                g = k * CPT + c
                zb = ZCH[g % 2]
                xnb = XNCH[c % 2]
                op("vector", lambda e, g=g, zb=zb, xb=XCH[c % 4]:
                   e.tensor_scalar(zb[:, :], xb[:, :], MUT[:, g:g + 1],
                                   RPT[:, g:g + 1], ALU.subtract, ALU.mult),
                   reads=(("XAG", g), ("MUT",), ("RPT",)),
                   writes=(("ZCH", g % 2),))
                plan.alias(("ZAG", g), ("ZCH", g % 2))
                op("scalar", lambda e, g=g, xnb=xnb, xb=XCH[c % 4]:
                   e.activation(xnb[:, :], xb[:, :], AF.Identity,
                                bias=XBB[:, g:g + 1], scale=RT[:, g:g + 1]),
                   reads=(("XAG", g), ("VA",), ("RT",)),
                   writes=(("WCSF", c % 2),))
                plan.alias(("XNG", g), ("WCSF", c % 2))
                emit_transpose(
                    lambda dc, zb=zb: zb[:, dc * 128:(dc + 1) * 128],
                    lambda dc, g=g: ("ZAG", g),
                    lambda dc, c=c, k=k: ZT[k % 2][:, dc * TT + c * 128:
                                                   dc * TT + (c + 1) * 128],
                    lambda dc, c=c, k=k: ("ZTp", k % 2, dc, c),
                    DC)
                emit_transpose(
                    lambda dc, xnb=xnb: xnb[:, dc * 128:(dc + 1) * 128],
                    lambda dc, g=g: ("XNG", g),
                    lambda dc, c=c, k=k: XNT2[k % 2][:, dc * TMEGA + c * 128:
                                                     dc * TMEGA + (c + 1) * 128],
                    lambda dc, c=c, k=k: ("XNTp", k % 2, dc, c),
                    DC)

        def xnt_keys_of(k):
            return tuple(("XNTp", k % 2, dc, tc)
                         for dc in range(DC) for tc in range(NTC))

        def round_gate(k):
            XNT = XNT2[k % 2]
            CWM = CWM2[k % 2]
            xk = xnt_keys_of(k)
            def _gmm(e, XNT=XNT):
                last = None
                for tc in range(NTC):
                    for dc in range(DC):
                        last = e.matmul(
                            PS_C[:, tc * 8:(tc + 1) * 8],
                            XNT[:, dc * TMEGA + tc * 128:
                                dc * TMEGA + (tc + 1) * 128],
                            GW16[:, dc * 8:(dc + 1) * 8],
                            start=(dc == 0), stop=(dc == DC - 1))
                return last
            op("tensor", _gmm, reads=xk + (("GW16",),), writes=(("PSC", 0),))
            op("vector", lambda e: e.tensor_tensor(GT0[:, :], PS_C[:, 0:8 * NTC],
               GBR[:, :], ALU.add), reads=(("PSC", 0), ("GBR",)),
               writes=(("GT0",),) + tuple(("ETB", t2) for t2 in range(NTC)))
            gt3 = GT0[:, :].rearrange("p (t e) -> p t e", e=8)
            op("vector", lambda e: e.tensor_reduce(MX[:, :], gt3, AX.X, ALU.max),
               reads=(("GT0",),), writes=(("MX",),))
            op("vector", lambda e: e.tensor_scalar(NMX[:, :], MX[:, :], -1.0,
               None, ALU.mult), reads=(("MX",),), writes=(("NMX",),))
            for tc in range(NTC):
                op("scalar", lambda e, tc=tc: e.activation(
                    ETB[:, tc * 8:(tc + 1) * 8], GT0[:, tc * 8:(tc + 1) * 8],
                    AF.Exp, bias=NMX[:, tc:tc + 1]),
                   reads=(("GT0",), ("NMX",)), writes=(("ETB", tc),))
            et3 = ETB[:, :].rearrange("p (t e) -> p t e", e=8)
            etk = tuple(("ETB", tc) for tc in range(NTC))
            op("vector", lambda e: e.tensor_reduce(SMS[:, :], et3, AX.X, ALU.add),
               reads=etk, writes=(("SMS",),))
            op("vector", lambda e: e.reciprocal(SIV[:, :], SMS[:, :]),
               reads=(("SMS",),), writes=(("SIV",),))
            op("vector", lambda e: e.tensor_reduce(M1[:, :], et3, AX.X, ALU.max),
               reads=etk, writes=(("M1",),))
            for tc in range(NTC):
                esl = slice(tc * 8, (tc + 1) * 8)
                op("vector", lambda e, tc=tc, esl=esl: e.scalar_tensor_tensor(
                    PMT[:, :], ETB[:, esl], M1[:, tc:tc + 1], ETB[:, esl],
                    ALU.is_lt, ALU.mult),
                   reads=(("ETB", tc), ("M1",)), writes=(("PMT",),))
                op("vector", lambda e, tc=tc: e.tensor_reduce(
                    M2[:, tc:tc + 1], PMT[:, :], AX.X, ALU.max),
                   reads=(("PMT",),), writes=(("M2", tc),))
                op("vector", lambda e, tc=tc, esl=esl: e.scalar_tensor_tensor(
                    CWU[:, :], ETB[:, esl], M2[:, tc:tc + 1], ETB[:, esl],
                    ALU.is_ge, ALU.mult),
                   reads=(("ETB", tc), ("M2", tc)), writes=(("PMT",),))
                op("vector", lambda e, tc=tc, esl=esl, CWM=CWM: e.tensor_scalar(
                    CWM[:, esl], CWU[:, :], SIV[:, tc:tc + 1], None, ALU.mult),
                   reads=(("PMT",), ("SIV",)), writes=(("CWMc", k % 2, tc),))

        def round_cwtr(k):
            CWM = CWM2[k % 2]
            CWAUG = CWAUG2[k % 2]
            for tc in range(NTC):
                esl = slice(tc * 8, (tc + 1) * 8)
                psb = PS_A if tc % 2 == 0 else PS_B
                pkk = ("PSA", 0) if tc % 2 == 0 else ("PSB", 0)
                op("tensor", lambda e, esl=esl, CWM=CWM, psb=psb: e.transpose(
                    psb[0:8, 128:256], CWM[:, esl], ID32[:, :]),
                   reads=(("CWMc", k % 2, tc), ("ID32",)), writes=(pkk,))
                cwsb = CWSB2[tc % 2]
                op("scalar", lambda e, psb=psb, cwsb=cwsb: e.activation(
                    cwsb[:, :], psb[0:8, 128:256], AF.Copy),
                   reads=(pkk,), writes=(("CWSB", 0),))
                op("gpsimd", lambda e, tc=tc, CWAUG=CWAUG, cwsb=cwsb:
                   e.tensor_copy(CWAUG[0:8, tc * 128:(tc + 1) * 128],
                                 cwsb[:, :]),
                   reads=(("CWSB", 0),), writes=(("CWAUGc", tc),))

        def expert_ffn(k, ex, shared):
            XNT = XNT2[k % 2]
            CWM = CWM2[k % 2]
            CWAUG = CWAUG2[k % 2]
            xk = xnt_keys_of(k)
            sws = SWS
            def load_pair(p, ex=ex):
                gp = gpair["i"]
                gpair["i"] += 1
                buf = W1P[gp % 3]
                pk = ("W1P", gp % 3)
                g0 = (ex * 3 * FC + 2 * p) * 128
                op("sync", lambda e,
                   o=buf[:, :].rearrange("r (h d) -> r h d", d=D),
                   s=wgt16[g0:g0 + 256, :].rearrange("(h r) d -> r h d", r=128):
                   e.dma_start(out=o, in_=s),
                   reads=(("WG", ex, 0), ("WG", ex, 1)),
                   writes=(pk,), dma=True, dsem=f"WP{gp % 3}")
                plan.alias(("W1Gp", k, ex, p), pk)
            load_pair(0)
            load_pair(1)
            for dh in range(2):
                g2 = (ex * 3 * FC + 2 * FC) * 128
                op("sync", lambda e, dh=dh,
                   o=W2E2[dh][:, :].rearrange("r (f d) -> r f d", d=DH),
                   s=wgt16[g2:g2 + FC * 128,
                           dh * DH:(dh + 1) * DH].rearrange(
                       "(f r) d -> r f d", r=128):
                   e.dma_start(out=o, in_=s),
                   reads=(("WG2", ex),),
                   writes=tuple(("W2E", dh, j) for j in range(FC)),
                   dma=True, dsem=f"WE{dh}")
            for p in range(FC):
                wb = W1P[(gpair["i"] - FC + p - 2 + FC) % 3] if False else None
                for th in range(NTH):
                    pa = PS_A[:, th * THW:(th + 1) * THW]
                    pb = PS_B[:, th * THW:(th + 1) * THW]
                    ka = ("PSA", th % 2)
                    kb = ("PSB", th % 2)
                    wbuf = W1P[(gpair["i"] - (FC - p)) % 3] if False else None
                    def _m1a(e, p=p, th=th, pa=pa, XNT=XNT, ex=ex, k=k):
                        wb_ = w1bufs[(k, ex, p)]
                        last = None
                        for dc in range(DC):
                            last = e.matmul(
                                pa, wb_[:, dc * 128:(dc + 1) * 128],
                                XNT[:, dc * TMEGA + th * THW:
                                    dc * TMEGA + (th + 1) * THW],
                                start=(dc == 0), stop=(dc == DC - 1))
                        return last
                    op("tensor", _m1a,
                       reads=xk + (("W1Gp", k, ex, p),), writes=(ka,))
                    def _m1b(e, p=p, th=th, pb=pb, XNT=XNT, ex=ex, k=k):
                        wb_ = w1bufs[(k, ex, p)]
                        last = None
                        for dc in range(DC):
                            last = e.matmul(
                                pb, wb_[:, D + dc * 128:D + (dc + 1) * 128],
                                XNT[:, dc * TMEGA + th * THW:
                                    dc * TMEGA + (th + 1) * THW],
                                start=(dc == 0), stop=(dc == DC - 1))
                        return last
                    op("tensor", _m1b,
                       reads=xk + (("W1Gp", k, ex, p),), writes=(kb,))
                    sb = SIL[th % 2]
                    op("scalar", lambda e, sb=sb, pa=pa, p=p, ex=ex:
                       e.activation(sb[:, :], pa, AF.Silu,
                                    bias=B1S[:, ex * 2 * FC + p:
                                             ex * 2 * FC + p + 1]),
                       reads=(ka, ("B1S",)), writes=(("SIL", th % 2),))
                    op("vector", lambda e, sb=sb, pb=pb, p=p, th=th, ex=ex:
                       e.scalar_tensor_tensor(
                           sws[:, p * TMEGA + th * THW:
                               p * TMEGA + (th + 1) * THW],
                           pb, B1S[:, ex * 2 * FC + FC + p:
                                   ex * 2 * FC + FC + p + 1],
                           sb[:, :], ALU.add, ALU.mult),
                       reads=(kb, ("SIL", th % 2), ("B1S",)),
                       writes=(("SWSp", p, th),))
                if p + 2 < FC:
                    load_pair(p + 2)
            sws_keys = tuple(("SWSp", p, th)
                             for p in range(FC) for th in range(NTH))
            hm_keys = tuple(("HM", ncj, 0) for ncj in range(NC))
            for tc in range(NTC):
                psb = PS_A if tc % 2 == 0 else PS_B
                nm_ = "PSA" if tc % 2 == 0 else "PSB"
                kc0 = (nm_, 0)
                kc1 = (nm_, (1 * DH) // PCW)
                if shared:
                    def _ymm(e, tc=tc, psb=psb, XNT=XNT, CWAUG=CWAUG):
                        last = None
                        for ncj in range(NC):
                            for dh in range(2):
                                last = e.matmul(
                                    psb[:, dh * DH:(dh + 1) * DH],
                                    HMEGA[:, ncj * TMEGA + tc * 128:
                                          ncj * TMEGA + (tc + 1) * 128],
                                    WC16[:, ncj * D + dh * DH:
                                         ncj * D + (dh + 1) * DH],
                                    start=(ncj == 0
                                           and (dh * DH) % 512 == 0),
                                    stop=False)
                        for dh in range(2):
                            last = e.matmul(
                                psb[:, dh * DH:(dh + 1) * DH],
                                CWAUG[0:9, tc * 128:(tc + 1) * 128],
                                RB2A16[0:9, dh * DH:(dh + 1) * DH],
                                start=False, stop=False)
                        for dc in range(DC):
                            last = e.matmul(
                                psb[:, dc * 128:(dc + 1) * 128],
                                XNT[:, dc * TMEGA + tc * 128:
                                    dc * TMEGA + (tc + 1) * 128],
                                DIAG16[:, dc * 128:(dc + 1) * 128],
                                start=False, stop=False)
                        for p in range(FC):
                            for dh in range(2):
                                last = e.matmul(
                                    psb[:, dh * DH:(dh + 1) * DH],
                                    sws[:, p * TMEGA + tc * 128:
                                        p * TMEGA + (tc + 1) * 128],
                                    W2E2[dh][:, p * DH:(p + 1) * DH],
                                    start=False,
                                    stop=(p == FC - 1
                                          and (DH >= 512 or dh == 1)))
                        return last
                    op("tensor", _ymm,
                       reads=hm_keys + sws_keys + xk
                       + tuple(("WC16", j) for j in range(NC))
                       + tuple(("W2E", dh, p) for dh in range(2)
                               for p in range(FC))
                       + tuple(("CWAUGc", t2) for t2 in range(NTC))
                       + tuple(("DIAG16", q) for q in range(DC))
                       + (("CWAUG", 8), ("RB2A", 0), ("RB2A", 8)),
                       writes=(kc0, kc1))
                    for dh in range(2):
                        op("vector", lambda e, tc=tc, dh=dh, psb=psb:
                           e.tensor_tensor(
                               OUTA[:, tc * D + dh * DH:tc * D + (dh + 1) * DH],
                               OUTA[:, tc * D + dh * DH:tc * D + (dh + 1) * DH],
                               psb[:, dh * DH:(dh + 1) * DH], ALU.add),
                           reads=((kc0, kc1)[dh], ("OUTA", tc)),
                           writes=(("OUTA", tc),))
                else:
                    def _m2(e, tc=tc, psb=psb):
                        last = None
                        for p in range(FC):
                            for dh in range(2):
                                last = e.matmul(
                                    psb[:, dh * DH:(dh + 1) * DH],
                                    sws[:, p * TMEGA + tc * 128:
                                        p * TMEGA + (tc + 1) * 128],
                                    W2E2[dh][:, p * DH:(p + 1) * DH],
                                    start=(p == 0 and (dh * DH) % 512 == 0),
                                    stop=(p == FC - 1
                                          and (DH >= 512 or dh == 1)))
                        return last
                    op("tensor", _m2,
                       reads=sws_keys
                       + tuple(("W2E", dh, p) for dh in range(2)
                               for p in range(FC)),
                       writes=(kc0, kc1))
                    for dh in range(2):
                        op("vector", lambda e, tc=tc, dh=dh, psb=psb, ex=ex,
                           CWM=CWM:
                           e.scalar_tensor_tensor(
                               OUTA[:, tc * D + dh * DH:tc * D + (dh + 1) * DH],
                               psb[:, dh * DH:(dh + 1) * DH],
                               CWM[:, tc * 8 + ex - 1:tc * 8 + ex],
                               OUTA[:, tc * D + dh * DH:tc * D + (dh + 1) * DH],
                               ALU.mult, ALU.add),
                           reads=((kc0, kc1)[dh], ("CWMc", k % 2, tc),
                                  ("OUTA", tc)),
                           writes=(("OUTA", tc),))

        def round_outa_init(k, eng="gpsimd"):
            t0 = k * TMEGA
            op(eng, lambda e, t0=t0: e.dma_start(
                out=OUTA[:, :].rearrange("p (tc d) -> p tc d", d=D),
                in_=x4[t0:t0 + TMEGA, :].rearrange("(tc p) d -> p tc d", p=128)),
               writes=tuple(("OUTA", tc) for tc in range(NTC)), dma=True)

        def round_store(k):
            t0 = k * TMEGA
            op("gpsimd", lambda e, t0=t0: e.dma_start(
                out=y4[t0:t0 + TMEGA, :].rearrange("(tc p) d -> p tc d", p=128),
                in_=OUTA[:, :].rearrange("p (tc d) -> p tc d", d=D)),
               reads=tuple(("OUTA", tc) for tc in range(NTC)),
               writes=(("Y4", k),), dma=True)

        w1bufs = {}
        _gp = 0
        for _k in range(NTA):
            for _exi in range(NE + 1):
                _ex = _exi + 1 if _exi < NE else 0
                for _p in range(FC):
                    w1bufs[(_k, _ex, _p)] = W1P[_gp % 3]
                    _gp += 1

        round_outa_init(0, eng="scalar")
        for k in range(NTA):
            round_S1(k)
            if k >= 1:
                for exi in range(NE):
                    expert_ffn(k - 1, exi + 1, False)
                round_cwtr(k - 1)
                carry_round(k - 1)
            round_gate(k)
            attn_round(k)
            if k >= 1:
                expert_ffn(k - 1, 0, True)
                round_store(k - 1)
                round_outa_init(k)
            round_pack(k)
        for exi in range(NE):
            expert_ffn(NTA - 1, exi + 1, False)
        round_cwtr(NTA - 1)
        carry_round(NTA - 1)
        expert_ffn(NTA - 1, 0, True)
        round_store(NTA - 1)

        plan.raw_wait("sync", DSEM["sync"], plan.cnt.get(DSEM["sync"], 0))
        plan.raw_wait("gpsimd", DSEM["gpsimd"], plan.cnt.get(DSEM["gpsimd"], 0))

        with nc.Block() as block:
            @block.sync
            def _(e):
                plan.run("sync", e, sems)

            @block.scalar
            def _(e):
                plan.run("scalar", e, sems)

            @block.vector
            def _(e):
                plan.run("vector", e, sems)

            @block.tensor
            def _(e):
                plan.run("tensor", e, sems)

            @block.gpsimd
            def _(e):
                plan.run("gpsimd", e, sems)

    return nc


def _prep_inputs(cfg, core, full):
    DC, TT, NTA, FC = cfg["DC"], cfg["TT"], cfg["NTA"], cfg["FC"]
    D = DC * 128
    TOK = TT * NTA
    NTC = cfg["TMEGA"] // 128
    b, pos = divmod(core, 4)
    NTA = cfg["NTA"]
    TTILE = TOK // NTA
    x = np.ascontiguousarray(np.concatenate(
        [full["x"][b, (k * 4 + pos) * TTILE:(k * 4 + pos + 1) * TTILE, :]
         for k in range(NTA)], axis=0))

    def chunked(v):
        return np.ascontiguousarray(v.reshape(DC, 128).T)

    w1s, w2s = [], []
    for e in range(9):
        w1 = full["w1"][e]
        w2 = full["w2"][e]
        w1s.append(np.ascontiguousarray(
            w1.reshape(DC, 128, 2 * FC, 128).transpose(2, 1, 0, 3)
            .reshape(2 * FC * 128, D)))
        w2s.append(np.ascontiguousarray(w2.reshape(FC * 128, D)))
    b1s = np.zeros((128, 9 * 2 * FC), np.float32)
    for e in range(9):
        b1s[:, e * 2 * FC:(e + 1) * 2 * FC] = \
            full["b1"][e].reshape(2 * FC, 128).T

    m8 = np.zeros((1, 8), np.float32)
    mb8 = np.zeros((1, 8), np.float32)
    for j in range(8):
        jb, jp = divmod(j, 4)
        if jb == b:
            mb8[0, j] = 1.0
            if jp < pos:
                m8[0, j] = 1.0
    m8r = np.tile(m8, (128, 1))
    mb8r = np.tile(mb8, (128, 1))

    import ml_dtypes
    return dict(
        x4=x,
        wa_t=np.ascontiguousarray(full["W_alpha_w"].T),
        wb_t=np.ascontiguousarray(full["W_B_w"].T),
        wc_t=np.ascontiguousarray(full["W_C_w"].T),
        gw_in=np.ascontiguousarray(
            full["gate_w"].reshape(DC, 128, 8).transpose(1, 0, 2)
            .reshape(128, DC * 8)),
        gb8_in=np.ascontiguousarray(np.tile(full["gate_b"][None, :], (1, NTC))),
        w1_all=np.concatenate(w1s, 0),
        w2_all=np.concatenate(w2s, 0),
        b1s_in=b1s,
        rb2_in=np.ascontiguousarray(full["rb2"]),
        wcb_in=full["W_C_b"][None, :].copy(),
        sb2_in=full["sb2"][None, :].copy(),
        alb_in=chunked(full["W_alpha_b"]),
        bbb_in=chunked(full["W_B_b"]),
        lnw_in=chunked(full["ln_w"]),
        lnb_in=chunked(full["ln_b"]),
        dpar_in=chunked(full["D_param"]),
        m8r_in=np.ascontiguousarray(m8r),
        mc8r_in=np.ascontiguousarray(1.0 - m8r),
        mb8r_in=np.ascontiguousarray(mb8r),
        mbc8r_in=np.ascontiguousarray(1.0 - mb8r),
        mi_in=np.tile(np.repeat(m8r, 2, axis=1),
                      (1, DC)).astype(ml_dtypes.bfloat16),
        mic_in=(np.tile(np.repeat(1.0 - m8r, 2, axis=1)
                        * np.tile(np.array([1.0, 0.0], np.float32), 8),
                        (1, DC))).astype(ml_dtypes.bfloat16),
        mbi_in=np.tile(np.repeat(mb8r, 2, axis=1),
                       (1, DC)).astype(ml_dtypes.bfloat16),
        mbic_in=(np.tile(np.repeat(1.0 - mb8r, 2, axis=1)
                         * np.tile(np.array([1.0, 0.0], np.float32), 8),
                         (1, DC))).astype(ml_dtypes.bfloat16),
        id16_in=np.eye(128, dtype=ml_dtypes.bfloat16),
        id32_in=np.eye(128, dtype=np.float32),
    )


def kernel(L0_attn_x, L0_attn_W_alpha_w, L0_attn_W_alpha_b, L0_attn_W_B_w,
           L0_attn_W_B_b, L0_attn_W_C_w, L0_attn_W_C_b, L0_attn_D_param,
           L0_attn_ln_w, L0_attn_ln_b, L0_ffn_gate_w, L0_ffn_gate_b,
           L0_ffn_sw1, L0_ffn_sb1, L0_ffn_sw2, L0_ffn_sb2, L0_ffn_rw1,
           L0_ffn_rb1, L0_ffn_rw2, L0_ffn_rb2):
    cfg = REAL_CFG
    D = cfg["DC"] * 128
    TOK = cfg["TT"] * cfg["NTA"]
    x = np.asarray(L0_attn_x, np.float32)
    full = dict(
        x=x,
        W_alpha_w=np.asarray(L0_attn_W_alpha_w, np.float32),
        W_alpha_b=np.asarray(L0_attn_W_alpha_b, np.float32),
        W_B_w=np.asarray(L0_attn_W_B_w, np.float32),
        W_B_b=np.asarray(L0_attn_W_B_b, np.float32),
        W_C_w=np.asarray(L0_attn_W_C_w, np.float32),
        W_C_b=np.asarray(L0_attn_W_C_b, np.float32),
        D_param=np.asarray(L0_attn_D_param, np.float32),
        ln_w=np.asarray(L0_attn_ln_w, np.float32),
        ln_b=np.asarray(L0_attn_ln_b, np.float32),
        gate_w=np.asarray(L0_ffn_gate_w, np.float32),
        gate_b=np.asarray(L0_ffn_gate_b, np.float32),
        w1=[np.asarray(L0_ffn_sw1[0], np.float32)]
           + [np.asarray(L0_ffn_rw1[e], np.float32) for e in range(8)],
        w2=[np.asarray(L0_ffn_sw2[0], np.float32)]
           + [np.asarray(L0_ffn_rw2[e], np.float32) for e in range(8)],
        b1=[np.asarray(L0_ffn_sb1[0], np.float32)]
           + [np.asarray(L0_ffn_rb1[e], np.float32) for e in range(8)],
        rb2=np.asarray(L0_ffn_rb2, np.float32),
        sb2=np.asarray(L0_ffn_sb2[0], np.float32),
    )
    nc = build_nc(cfg)
    in_maps = [_prep_inputs(cfg, c, full) for c in range(8)]
    res = run_bass_kernel_spmd(nc, in_maps, core_ids=list(range(8)))
    B, T = x.shape[0], x.shape[1]
    out = np.empty((B, T, D), np.float32)
    NTA = cfg["NTA"]
    TTILE = TOK // NTA
    for c in range(8):
        b, pos = divmod(c, 4)
        y = res.results[c]["y4"]
        for k in range(NTA):
            out[b, (k * 4 + pos) * TTILE:(k * 4 + pos + 1) * TTILE, :] = \
                y[k * TTILE:(k + 1) * TTILE, :]
    return out


# version 20
# speedup vs baseline: 1.0379x; 1.0379x over previous
import contextlib
import numpy as np

import concourse.bass as bass
import concourse.mybir as mybir
from concourse.bass_utils import run_bass_kernel_spmd

F32 = mybir.dt.float32
BF16 = mybir.dt.bfloat16
AF = mybir.ActivationFunctionType
ALU = mybir.AluOpType
AX = mybir.AxisListType

ENGINES = ("sync", "scalar", "vector", "tensor", "gpsimd")
PSEM = {e: "P_" + e for e in ENGINES}
DSEM = {e: "D_" + e for e in ENGINES}

REAL_CFG = dict(DC=8, TT=512, NTA=8, TMEGA=512, NMEGA=8, FC=8)


class Plan:

    def __init__(self):
        self.q = {e: [] for e in ENGINES}
        self.cnt = {}
        self.clock = {e: {} for e in ENGINES}
        self.buf = {}

    def _wait(self, eng, sem, val):
        if val <= 0 or self.clock[eng].get(sem, 0) >= val:
            return
        self.clock[eng][sem] = val
        self.q[eng].append(("w", sem, val))

    def alias(self, new_key, phys_key):
        self.buf[new_key] = self.buf[phys_key]

    def op(self, eng, fn, reads=(), writes=(), dma=False, cc=False,
           dsem=None):
        if cc:
            sem, amt = "CC", None
        elif dma:
            if dsem is not None:
                sem, amt = dsem, 16
            else:
                sem, amt = DSEM[eng], 16
                self._wait(eng, sem, self.cnt.get(sem, 0))
        else:
            sem, amt = PSEM[eng], 1
        for k in reads:
            st = self.buf.get(k)
            if st is None:
                continue
            for s, v in st["w"].items():
                if eng == "tensor" and s == PSEM["tensor"]:
                    continue
                self._wait(eng, s, v)
        for k in writes:
            st = self.buf.get(k)
            if st is not None:
                for s, v in st["w"].items():
                    if eng == "tensor" and s == PSEM["tensor"]:
                        continue
                    self._wait(eng, s, v)
                for s, v in st["r"].items():
                    if eng == "tensor" and s == PSEM["tensor"]:
                        continue
                    self._wait(eng, s, v)
        newv = self.cnt.get(sem, 0) + (1 if cc else (amt or 1))
        self.cnt[sem] = newv
        self.q[eng].append(("o", fn, sem, (None if cc else amt)))
        for k in writes:
            self.buf[k] = {"w": {sem: newv}, "r": {}}
        for k in reads:
            st = self.buf.get(k)
            if st is None:
                st = self.buf[k] = {"w": {}, "r": {}}
            st["r"][sem] = max(st["r"].get(sem, 0), newv)
        return newv

    def raw_wait(self, eng, sem, val):
        self._wait(eng, sem, val)

    def run(self, eng, handle, sems):
        for item in self.q[eng]:
            if item[0] == "w":
                handle.wait_ge(sems[item[1]], item[2])
            else:
                _, fn, sem, amt = item
                ins = fn(handle)
                if amt is None:
                    ins.then_inc(sems[sem])
                else:
                    ins.then_inc(sems[sem], amt)


def build_nc(cfg):
    DC = cfg["DC"]
    D = DC * 128
    NC = DC
    TT = cfg["TT"]
    CPT = TT // 128
    NTA = cfg["NTA"]
    TOK = TT * NTA
    TMEGA = cfg["TMEGA"]
    NTT = TMEGA // TT
    NTC = TMEGA // 128
    NMEGA = cfg["NMEGA"]
    FC = cfg["FC"]
    NE = 8
    DH = D // 2
    NDH = DH // 128
    THW = min(TMEGA, 512)
    NTH = TMEGA // THW
    EPS = 1e-5
    assert TT <= 512 and DH <= 512 and THW <= 512 and TOK == NMEGA * TMEGA

    nc = bass.Bass("TRN2", target_bir_lowering=False)

    def I(name, shape, dt=F32):
        return nc.dram_tensor(name, list(shape), dt, kind="ExternalInput").ap()

    x4 = I("x4", (TOK, D))
    wa_t = I("wa_t", (D, D))
    wb_t = I("wb_t", (D, D))
    wc_t = I("wc_t", (D, D))
    gw_in = I("gw_in", (128, DC * 8))
    gb8_in = I("gb8_in", (1, 8 * NTC))
    w1_all = I("w1_all", ((NE + 1) * 2 * FC * 128, D))
    w2_all = I("w2_all", ((NE + 1) * FC * 128, D))
    b1s_in = I("b1s_in", (128, (NE + 1) * 2 * FC))
    rb2_in = I("rb2_in", (NE, D))
    wcb_in = I("wcb_in", (1, D))
    sb2_in = I("sb2_in", (1, D))
    alb_in = I("alb_in", (128, NC))
    bbb_in = I("bbb_in", (128, NC))
    lnw_in = I("lnw_in", (128, DC))
    lnb_in = I("lnb_in", (128, DC))
    dpar_in = I("dpar_in", (128, DC))
    m8r_in = I("m8r_in", (128, 8))
    mc8r_in = I("mc8r_in", (128, 8))
    mb8r_in = I("mb8r_in", (128, 8))
    mbc8r_in = I("mbc8r_in", (128, 8))
    mi_in = I("mi_in", (128, 16 * NC), BF16)
    mic_in = I("mic_in", (128, 16 * NC), BF16)
    mbi_in = I("mbi_in", (128, 16 * NC), BF16)
    mbic_in = I("mbic_in", (128, 16 * NC), BF16)
    id16_in = I("id16_in", (128, 128), BF16)
    id32_in = I("id32_in", (128, 128))

    y4 = nc.dram_tensor("y4", [TOK, D], F32, kind="ExternalOutput").ap()

    cc_in = nc.dram_tensor("cc_in", [2, D], F32)
    cc_out = nc.dram_tensor("cc_out", [16, D], F32)
    wgt16 = nc.dram_tensor("wgt16", [(NE + 1) * 3 * FC * 128, D], BF16)

    plan = Plan()

    with contextlib.ExitStack() as es:
        def SB(name, shape, dt):
            return es.enter_context(nc.sbuf_tensor(name, list(shape), dt))
        def PSUM(name, shape, dt):
            return es.enter_context(nc.psum_tensor(name, list(shape), dt))

        WA16 = SB("WA16", (128, DC * D), BF16)
        WB16 = SB("WB16", (128, DC * D), BF16)
        WC16 = SB("WC16", (128, NC * D), BF16)
        GW16 = SB("GW16", (128, DC * 8), BF16)
        GB8R = SB("GB8R", (1, 8 * NTC), F32)
        GBR = SB("GBR", (128, 8 * NTC), F32)
        DIAG16 = SB("DIAG16", (128, D), BF16)
        DRC = SB("DRC", (128, DC), F32)
        LNW = SB("LNW", (128, DC), F32)
        LNB = SB("LNB", (128, DC), F32)
        ALB = SB("ALB", (128, NC), F32)
        BBB = SB("BBB", (128, NC), F32)
        BBB16 = SB("BBB16", (1, D), BF16)
        B1S = SB("B1S", (128, (NE + 1) * 2 * FC), F32)
        RB2A16 = SB("RB2A16", (9, D), BF16)
        ONES16 = SB("ONES16", (1, max(TT, TMEGA)), BF16)
        ONE1 = SB("ONE1", (1, 128), F32)
        EPSC = SB("EPSC", (128, 1), F32)
        NHC = SB("NHC", (128, 1), F32)
        NE1C = SB("NE1C", (128, 1), F32)
        ID16 = SB("ID16", (128, 128), BF16)
        ID32 = SB("ID32", (128, 128), F32)
        MI = SB("MI", (128, 16 * NC), BF16)
        MIC = SB("MIC", (128, 16 * NC), BF16)
        MBI = SB("MBI", (128, 16 * NC), BF16)
        MBIC = SB("MBIC", (128, 16 * NC), BF16)
        FCAR = SB("FCAR", (128, 16 * NC), F32)
        FG = SB("FG", (128, 16 * NC), F32)
        CFO = SB("CFO", (128, 8 * NC), F32)
        GFO = [SB("GFO0", (128, 8 * NC), F32),
               SB("GFO1", (128, 8 * NC), F32)]
        ATOT = SB("ATOT", (128, NC), F32)
        HLAST = SB("HLAST", (128, NC), F32)
        NSC = TOK // 128
        S1T = SB("S1T", (128, NSC), F32)
        S2T = SB("S2T", (128, NSC), F32)
        MUT = SB("MUT", (128, NSC), F32)
        RT = SB("RT", (128, NSC), F32)
        RPT = SB("RPT", (128, NSC), F32)
        VT = SB("VT", (128, NSC), F32)
        VA = SB("VA", (128, NSC), F32)
        VB = SB("VB", (128, NSC), F32)
        ZBB = VT
        XBB = VA
        XCH = [SB(f"XCH{i}", (128, D), F32) for i in range(4)]
        ZCH = [SB("ZCH0", (128, D), BF16), SB("ZCH1", (128, D), BF16)]
        SQD = ZCH[0]
        CCB = XCH[0]
        ZT = [SB("ZT0", (128, DC * TT), BF16), SB("ZT1", (128, DC * TT), BF16)]
        ALS = [SB("ALS0", (128, TT), F32), SB("ALS1", (128, TT), F32)]
        HS = [SB("HS0", (128, TT), F32), SB("HS1", (128, TT), F32)]
        WCS = [SB("WCS0", (128, D), BF16), SB("WCS1", (128, D), BF16)]
        XNCH = WCS
        BT0 = XCH[0]
        BT1 = XCH[1]
        OUTA = SB("OUTA", (128, NTC * D), F32)
        NWST = 2 if NTC >= 2 else 1
        WST = [OUTA[:, 0:D], OUTA[:, D:2 * D] if NTC >= 2 else OUTA[:, 0:D]]
        XNT2 = [SB("XNT0", (128, DC * TMEGA), BF16),
                SB("XNT1", (128, DC * TMEGA), BF16)]
        HMEGA = SB("HMEGA", (128, NC * TMEGA), BF16)
        HLOC = SB("HLOC", (128, NC * TMEGA), BF16)
        PLOC = SB("PLOC", (128, NC * TMEGA), BF16)

        SWS = SB("SWS", (128, FC * TMEGA), BF16)
        W1P = [SB(f"W1P{i}", (128, 2 * D), BF16) for i in range(3)]
        W2E2 = [SB("W2E0", (128, FC * DH), BF16),
                SB("W2E1", (128, FC * DH), BF16)]
        SIL = [SB("SIL0", (128, THW), F32), SB("SIL1", (128, THW), F32)]
        GT0 = SB("GT0", (128, 8 * NTC), F32)
        ETB = GT0
        MX = SB("MX", (128, NTC), F32)
        NMX = SB("NMX", (128, NTC), F32)
        SMS = SB("SMS", (128, NTC), F32)
        SIV = SB("SIV", (128, NTC), F32)
        M1 = SB("M1", (128, NTC), F32)
        M2 = SB("M2", (128, NTC), F32)
        PMT = SB("PMT", (128, 8), F32)
        CWU = PMT
        CWM2 = [SB("CWM0", (128, 8 * NTC), F32),
                SB("CWM1", (128, 8 * NTC), F32)]
        CWAUG1T = SB("CWAUG", (9, TMEGA), BF16)
        CWAUG2 = [CWAUG1T, CWAUG1T]
        CWSB = SB("CWSB", (8, 128), F32)
        CWSB2 = [CWSB, CWSB]
        AJH = SB("AJH", (128, 16 * NC), F32)

        PSW = max(TMEGA, D, 512)
        PS_A = PSUM("PS_A", (128, PSW), F32)
        PS_B = PSUM("PS_B", (128, PSW), F32)
        PS_C = PSUM("PS_C", (128, PSW), F32)
        PS_T1 = PSUM("PS_T1", (128, 1024), BF16)
        PS_T2 = PSUM("PS_T2", (128, 1024), BF16)
        PCW = PSW // 2

        sem_names = ([PSEM[e] for e in ENGINES] + [DSEM[e] for e in ENGINES]
                     + ["CC"] + [f"XS{i}" for i in range(4)]
                     + ["WP0", "WP1", "WP2", "WE0", "WE1", "WS0", "WS1"])
        sems = {n: es.enter_context(nc.semaphore(n)) for n in sem_names}

        op = plan.op

        def stats_chain(ncols):
            sl = slice(0, ncols)
            rs1 = [("S1T", c) for c in range(ncols)]
            rs2 = [("S2T", c) for c in range(ncols)]
            op("vector", lambda e: e.tensor_scalar(MUT[:, sl], S1T[:, sl],
               1.0 / D, None, ALU.mult), reads=rs1, writes=(("MUT",),))
            op("vector", lambda e: e.tensor_scalar(VA[:, sl], S2T[:, sl],
               1.0 / D, None, ALU.mult), reads=rs2, writes=(("VA",),))
            op("vector", lambda e: e.tensor_tensor(VB[:, sl], MUT[:, sl],
               MUT[:, sl], ALU.mult), reads=(("MUT",),), writes=(("VB",),))
            op("vector", lambda e: e.tensor_tensor(VT[:, sl], VA[:, sl],
               VB[:, sl], ALU.subtract),
               reads=(("VA",), ("VB",)), writes=(("VT",),))
            op("scalar", lambda e: e.activation(VA[:, sl], VT[:, sl], AF.Ln,
               bias=EPSC[:, 0:1]), reads=(("VT",), ("EPSC",)), writes=(("VA",),))
            op("scalar", lambda e: e.activation(RT[:, sl], VA[:, sl], AF.Exp,
               scale=NHC[:, 0:1]), reads=(("VA",), ("NHC",)), writes=(("RT",),))
            op("scalar", lambda e: e.activation(VB[:, sl], VA[:, sl], AF.Exp,
               scale=NE1C[:, 0:1]), reads=(("VA",), ("NE1C",)), writes=(("VB",),))
            op("vector", lambda e: e.tensor_tensor(VB[:, sl], VT[:, sl],
               VB[:, sl], ALU.mult), reads=(("VT",), ("VB",)), writes=(("VB",),))
            op("scalar", lambda e: e.activation(VB[:, sl], VB[:, sl], AF.Ln,
               bias=EPSC[:, 0:1]), reads=(("VB",), ("EPSC",)), writes=(("VB",),))
            op("scalar", lambda e: e.activation(VB[:, sl], VB[:, sl], AF.Exp,
               scale=NHC[:, 0:1]), reads=(("VB",), ("NHC",)), writes=(("VB",),))
            op("vector", lambda e: e.tensor_tensor(RPT[:, sl], RT[:, sl],
               VB[:, sl], ALU.mult), reads=(("RT",), ("VB",)), writes=(("RPT",),))
            op("vector", lambda e: e.tensor_tensor(ZBB[:, sl], MUT[:, sl],
               RPT[:, sl], ALU.mult), reads=(("MUT",), ("RPT",)),
               writes=(("VT",),))
            op("vector", lambda e: e.tensor_scalar(ZBB[:, sl], ZBB[:, sl],
               -1.0, None, ALU.mult), reads=(("ZBB",),), writes=(("VT",),))
            op("vector", lambda e: e.tensor_tensor(XBB[:, sl], MUT[:, sl],
               RT[:, sl], ALU.mult), reads=(("MUT",), ("RT",)),
               writes=(("VA",),))
            op("vector", lambda e: e.tensor_scalar(XBB[:, sl], XBB[:, sl],
               -1.0, None, ALU.mult), reads=(("XBB",),), writes=(("VA",),))

        tr_state = {"slot": 0}
        def emit_transpose(src_ap_fn, src_key_fn, dst_ap_fn, dst_key_fn, n):
            for i in range(n):
                slot = tr_state["slot"]
                tr_state["slot"] = slot + 1
                psb = PS_T1 if slot % 2 == 0 else PS_T2
                pkey = ("PSTR", slot % 2)
                pap = psb[:, 0:128]
                op("tensor", lambda e, pap=pap, s=src_ap_fn(i):
                   e.transpose(pap, s, ID16[:, :]),
                   reads=(src_key_fn(i), ("ID16",)), writes=(pkey,))
                op("scalar", lambda e, pap=pap, d=dst_ap_fn(i):
                   e.activation(d, pap, AF.Copy),
                   reads=(pkey,), writes=(dst_key_fn(i),))

        def attn_round(k):
            zt = ZT[k % 2]
            ztile_keys = tuple(("ZTp", k % 2, dc, c)
                               for dc in range(DC) for c in range(CPT))
            for ncj in range(NC):
                pa = PS_A[:, (ncj % 2) * TT:((ncj % 2) + 1) * TT]
                pu = PS_B[:, (ncj % 2) * TT:((ncj % 2) + 1) * TT]
                ka = ("PSA", ncj % 2)
                ku = ("PSB", ncj % 2)
                def _amm(e, ncj=ncj, pa=pa, zt=zt):
                    last = None
                    for dc in range(DC):
                        last = e.matmul(pa, WA16[:, dc * D + ncj * 128:
                                                 dc * D + (ncj + 1) * 128],
                                        zt[:, dc * TT:(dc + 1) * TT],
                                        start=(dc == 0), stop=(dc == DC - 1))
                    return last
                op("tensor", _amm,
                   reads=ztile_keys + tuple(("W16", 0, dc) for dc in range(DC)),
                   writes=(ka,))
                def _umm(e, ncj=ncj, pu=pu, zt=zt):
                    for dc in range(DC):
                        e.matmul(pu, WB16[:, dc * D + ncj * 128:
                                          dc * D + (ncj + 1) * 128],
                                 zt[:, dc * TT:(dc + 1) * TT],
                                 start=(dc == 0), stop=False)
                    return e.matmul(pu, BBB16[0:1, ncj * 128:(ncj + 1) * 128],
                                    ONES16[0:1, 0:TT], start=False, stop=True)
                op("tensor", _umm,
                   reads=ztile_keys + tuple(("W16", 1, dc) for dc in range(DC))
                   + (("BBB16", ncj), ("ONES16",)),
                   writes=(ku,))
                ab = ALS[ncj % 2]
                op("scalar", lambda e, ab=ab, pa=pa, ncj=ncj: e.activation(
                    ab[:, :], pa, AF.Sigmoid, bias=ALB[:, ncj:ncj + 1]),
                   reads=(ka, ("ALB",)), writes=(("ALS", ncj % 2),))
                hb = HS[ncj % 2]
                op("vector", lambda e, hb=hb, ab=ab, pu=pu:
                   e.tensor_tensor_scan(hb[:, :], ab[:, :], pu, 0.0,
                                        ALU.mult, ALU.add),
                   reads=(("ALS", ncj % 2), ku), writes=(("HS", ncj % 2),))
                op("vector", lambda e, ncj=ncj, ab=ab:
                   e.tensor_tensor_scan(
                       PLOC[:, ncj * TMEGA:(ncj + 1) * TMEGA],
                       ab[:, :], ab[:, :], 1.0, ALU.mult, ALU.bypass),
                   reads=(("ALS", ncj % 2),), writes=(("PLOC", ncj),))
                op("vector", lambda e, hb=hb, ncj=ncj: e.tensor_copy(
                    HLOC[:, ncj * TMEGA:(ncj + 1) * TMEGA], hb[:, :]),
                   reads=(("HS", ncj % 2),), writes=(("HLOC", ncj),))
                op("vector", lambda e, hb=hb, ncj=ncj: e.tensor_copy(
                    HLAST[:, ncj:ncj + 1], hb[:, TT - 1:TT]),
                   reads=(("HS", ncj % 2),), writes=(("HLAST", ncj),))
                op("vector", lambda e, ab=ab, ncj=ncj: e.tensor_reduce(
                    ATOT[:, ncj:ncj + 1], ab[:, :], AX.X, ALU.mult),
                   reads=(("ALS", ncj % 2),), writes=(("ATOT", ncj),))
        def round_pack(k):
            for which, tile, keys in (
                    (0, ATOT, tuple(("ATOT", j) for j in range(NC))),
                    (1, HLAST, tuple(("HLAST", j) for j in range(NC)))):
                op("tensor", lambda e, t=tile: e.transpose(
                    PS_C[0:NC, 0:128], t[:, :], ID32[:, :]),
                   reads=keys + (("ID32",),), writes=(("PSC", 0),))
                op("scalar", lambda e: e.activation(SUMR[0:NC, :],
                   PS_C[0:NC, 0:128], AF.Copy),
                   reads=(("PSC", 0),), writes=(("CWSB", 0),))
                op("scalar", lambda e, w=which: e.dma_start(
                    out=cc_in[w:w + 1, :].rearrange("o (c p) -> o c p", p=128),
                    in_=SUMR[0:NC, :]),
                   reads=(("CWSB", 0),), writes=(("CCI", which),), dma=True)
            op("gpsimd", lambda e: e.collective_compute(
                "AllGather", ALU.bypass, replica_groups=[list(range(8))],
                ins=[cc_in.ap().opt()], outs=[cc_out.ap().opt()]),
               reads=(("CCI", 0), ("CCI", 1)), writes=(("CCO",),), cc=True)

        def carry_round(k):
            op("scalar", lambda e: e.dma_start(out=CCB[0:16, :],
               in_=cc_out[:, :]),
               reads=(("CCO",),), writes=(("XCH", 0),), dma=True)
            plan.alias(("CCB",), ("XCH", 0))
            for c in range(NC):
                psb = PS_A if c % 2 == 0 else PS_B
                pkk = ("PSA", 0) if c % 2 == 0 else ("PSB", 0)
                op("tensor", lambda e, c=c, psb=psb: e.transpose(
                    psb[0:128, 0:16], CCB[0:16, c * 128:(c + 1) * 128],
                    ID32[0:16, 0:16]),
                   reads=(("CCB",), ("ID32",)), writes=(pkk,))
                op("scalar", lambda e, c=c, psb=psb: e.activation(
                    AJH[:, c * 16:(c + 1) * 16], psb[0:128, 0:16],
                    AF.Copy),
                   reads=(pkk,), writes=(("AJH", c),))
            ajhk = tuple(("AJH", c) for c in range(NC))
            gs, gd = (k + 1) % 2, k % 2
            for (dst, dkey, mm, mmc) in ((FCAR, ("FCAR",), MI, MIC),
                                         (FG, ("FG",), MBI, MBIC)):
                op("vector", lambda e, dst=dst, mm=mm: e.tensor_tensor(
                    dst[:, :], AJH[:, :], mm[:, :], ALU.mult),
                   reads=ajhk + (("MI",), ("MBI",)), writes=(dkey,))
                op("vector", lambda e, dst=dst, mmc=mmc: e.tensor_tensor(
                    dst[:, :], dst[:, :], mmc[:, :], ALU.add),
                   reads=(dkey, ("MIC",), ("MBIC",)), writes=(dkey,))
            for c in range(NC):
                op("vector", lambda e, c=c, gs=gs: e.tensor_tensor_scan(
                    CFO[:, c * 8:(c + 1) * 8],
                    FCAR[:, c * 16:(c + 1) * 16].rearrange(
                        "p (j t) -> p j t", t=2)[:, :, 0],
                    FCAR[:, c * 16:(c + 1) * 16].rearrange(
                        "p (j t) -> p j t", t=2)[:, :, 1],
                    GFO[gs][:, c * 8 + 7:c * 8 + 8],
                    ALU.mult, ALU.add),
                   reads=(("FCAR",), ("GFO", gs, c)), writes=(("CFO", c),))
                op("vector", lambda e, c=c, gs=gs, gd=gd: e.tensor_tensor_scan(
                    GFO[gd][:, c * 8:(c + 1) * 8],
                    FG[:, c * 16:(c + 1) * 16].rearrange(
                        "p (j t) -> p j t", t=2)[:, :, 0],
                    FG[:, c * 16:(c + 1) * 16].rearrange(
                        "p (j t) -> p j t", t=2)[:, :, 1],
                    GFO[gs][:, c * 8 + 7:c * 8 + 8],
                    ALU.mult, ALU.add),
                   reads=(("FG",), ("GFO", gs, c)), writes=(("GFO", gd, c),))
            for ncj in range(NC):
                op("vector", lambda e, ncj=ncj: e.scalar_tensor_tensor(
                    HMEGA[:, ncj * TMEGA:(ncj + 1) * TMEGA],
                    PLOC[:, ncj * TMEGA:(ncj + 1) * TMEGA],
                    CFO[:, ncj * 8 + 7:ncj * 8 + 8],
                    HLOC[:, ncj * TMEGA:(ncj + 1) * TMEGA],
                    ALU.mult, ALU.add),
                   reads=(("PLOC", ncj), ("HLOC", ncj), ("CFO", ncj)),
                   writes=(("HM", ncj, 0),))

        wjobs = []
        for e_ in list(range(1, NE + 1)) + [0]:
            wjobs.append(("w1", e_))
            wjobs.append(("w2", e_))

        def emit_phaseW(jobs):
            for kind, e_ in jobs:
                if kind == "w1":
                    s0 = e_ * 2 * FC * 128
                    d0 = e_ * 3 * FC * 128
                    dblk = wgt16[d0:d0 + 2 * FC * 128, :].rearrange(
                        "(p q) d -> p q d", q=256)
                    for h in range(2):
                        src_ap = w1_all[s0 + h * FC * 128:
                                        s0 + (h + 1) * FC * 128, :].rearrange(
                            "(p r) d -> p r d", r=128)
                        dst_ap = dblk[:, h * 128:(h + 1) * 128, :]
                        op("gpsimd", lambda e, o=dst_ap, s=src_ap:
                           e.dma_start(out=o, in_=s),
                           writes=(("WG", e_, h),), dma=True)
                else:
                    s0 = e_ * FC * 128
                    src_ap = w2_all[s0:s0 + FC * 128, :]
                    d0 = (e_ * 3 * FC + 2 * FC) * 128
                    dst_ap = wgt16[d0:d0 + FC * 128, :]
                    op("gpsimd", lambda e, o=dst_ap, s=src_ap:
                       e.dma_start(out=o, in_=s),
                       writes=(("WG2", e_),), dma=True)

        def ld(dst_ap, src_ap, key, reads=()):
            op("sync", lambda e, o=dst_ap, i=src_ap: e.dma_start(out=o, in_=i),
               reads=reads, writes=(key,), dma=True)

        emit_phaseW(wjobs[:2])
        ld(LNW[:, :], lnw_in, ("LNW",))
        ld(LNB[:, :], lnb_in, ("LNB",))
        ld(DRC[:, :], dpar_in, ("DRC",))
        ld(MI[:, :], mi_in, ("MI",))
        ld(MIC[:, :], mic_in, ("MIC",))
        ld(MBI[:, :], mbi_in, ("MBI",))
        ld(MBIC[:, :], mbic_in, ("MBIC",))
        op("vector", lambda e: e.memset(GFO[1][:, :], 0.0),
           writes=tuple(("GFO", 1, c) for c in range(NC)))
        ld(ID16[:, :], id16_in, ("ID16",))
        ld(ID32[:, :], id32_in, ("ID32",))
        ld(B1S[:, :], b1s_in, ("B1S",))

        op("sync", lambda e: e.dma_start(out=OUTA[0:1, 0:D], in_=wcb_in),
           writes=(("OUTA", 0),), dma=True)
        plan.alias(("BT0",), ("OUTA", 0))

        ld(GB8R[:, :], gb8_in, ("GB8R",))
        ld(ALB[:, :], alb_in, ("ALB",))
        ld(BBB[:, :], bbb_in, ("BBB",))
        op("vector", lambda e: e.memset(ONES16[:, :], 1.0), writes=(("ONES16",),))
        op("vector", lambda e: e.memset(ONE1[:, :], 1.0), writes=(("ONE1",),))
        op("vector", lambda e: e.memset(EPSC[:, :], EPS), writes=(("EPSC",),))
        op("vector", lambda e: e.memset(NHC[:, :], -0.5), writes=(("NHC",),))
        op("vector", lambda e: e.memset(NE1C[:, :], -1.0), writes=(("NE1C",),))
        NSC = TOK // 128
        for gi in reversed(range(NSC)):
            r0 = gi * 128
            buf = XCH[gi % 4]
            op("scalar" if gi % 2 == 0 else "sync",
               lambda e, o=buf[:, :], s=x4[r0:r0 + 128, :]:
               e.dma_start(out=o, in_=s), writes=(("XCH", gi % 4),), dma=True,
               dsem=f"XS{gi % 4}")
            plan.alias(("XPP", gi), ("XCH", gi % 4))
            op("vector", lambda e, gi=gi, xb=buf: e.reduce_sum(
                S1T[:, gi:gi + 1], xb[:, :], AX.X),
               reads=(("XPP", gi),), writes=(("S1T", gi),))
            op("scalar", lambda e, gi=gi, xb=buf: e.activation(
                SQD[:, :], xb[:, :], AF.Square,
                accum_out=S2T[:, gi:gi + 1]),
               reads=(("XPP", gi),), writes=(("S2T", gi), ("ZCH", 0)))
        stats_chain(NSC)


        op("gpsimd", lambda e: e.dma_start(out=GW16[:, :], in_=gw_in),
           writes=(("GW16",),), dma=True)
        op("tensor", lambda e: e.matmul(PS_A[:, 0:8 * NTC], ONE1[0:1, :],
                                        GB8R[0:1, :], start=True, stop=True),
           reads=(("GB8R",), ("ONE1",)), writes=(("PSA", 0),))
        op("vector", lambda e: e.tensor_copy(GBR[:, :], PS_A[:, 0:8 * NTC]),
           reads=(("PSA", 0),), writes=(("GBR",),))
        for dc in range(DC):
            op("vector", lambda e, dc=dc: e.tensor_scalar(
                DIAG16[:, dc * 128:(dc + 1) * 128], ID16[:, :],
                DRC[:, dc:dc + 1], None, ALU.mult),
               reads=(("ID16",), ("DRC",)), writes=(("DIAG16", dc),))

        op("gpsimd", lambda e: e.dma_start(out=RB2A16[0:8, :], in_=rb2_in),
           writes=(("RB2A", 0),), dma=True)
        op("gpsimd", lambda e: e.dma_start(out=OUTA[0:1, 0:D], in_=sb2_in,
                                           accum_op=ALU.add),
           reads=(("BT0",),), writes=(("OUTA", 0),), dma=True)
        plan.alias(("BT0",), ("OUTA", 0))
        plan.alias(("BT0",), ("OUTA", 0))
        op("gpsimd", lambda e: e.dma_start(out=RB2A16[8:9, :],
                                           in_=OUTA[0:1, 0:D]),
           reads=(("BT0",),), writes=(("RB2A", 8),), dma=True)
        op("gpsimd", lambda e: e.dma_start(
            out=CWAUG1T[8:9, :], in_=ONES16[0:1, 0:TMEGA]),
           reads=(("ONES16",),), writes=(("CWAUG", 8),), dma=True)
        for wsel, (src, dst16) in enumerate([(wa_t, WA16), (wb_t, WB16)]):
            for dc in range(DC):
                buf = WST[dc % NWST]
                pk = ("OUTA", dc % NWST)
                op("sync", lambda e, o=buf,
                   i=src[dc * 128:(dc + 1) * 128, :]: e.dma_start(out=o, in_=i),
                   writes=(pk,), dma=True, dsem=f"WS{dc % NWST}")
                plan.alias(("WSTP", wsel, dc), pk)
                op("vector", lambda e, o=dst16[:, dc * D:(dc + 1) * D],
                   i=buf, s=LNW[:, dc:dc + 1]:
                   e.tensor_scalar(o, i, s, None, ALU.mult),
                   reads=(("WSTP", wsel, dc), ("LNW",)),
                   writes=(("W16", wsel, dc),))
                def _mv(e, dc=dc, wsel=wsel, buf=buf):
                    last = None
                    for ncj in range(NC):
                        col = PCW + wsel * NC + ncj
                        st = (wsel == 0 and dc == 0 and ncj == 0)
                        sp = (wsel == 1 and dc == DC - 1 and ncj == NC - 1)
                        last = e.matmul(PS_C[:, col:col + 1],
                                        buf[:, ncj * 128:(ncj + 1) * 128],
                                        LNB[:, dc:dc + 1],
                                        start=st, stop=sp)
                    return last
                buf = None
                op("tensor", _mv, reads=(("WSTP", wsel, dc), ("LNB",)),
                   writes=(("PSC", 1),))
        op("vector", lambda e: e.tensor_tensor(
            ALB[:, :], ALB[:, :], PS_C[:, PCW:PCW + NC], ALU.add),
           reads=(("ALB",), ("PSC", 1)), writes=(("ALB",),))
        op("vector", lambda e: e.tensor_tensor(
            BBB[:, :], BBB[:, :], PS_C[:, PCW + NC:PCW + 2 * NC], ALU.add),
           reads=(("BBB",), ("PSC", 1)), writes=(("BBB",),))
        for ncj in range(NC):
            op("gpsimd", lambda e, ncj=ncj: e.dma_start(
                out=BBB16[0:1, ncj * 128:(ncj + 1) * 128],
                in_=BBB[:, ncj:ncj + 1]),
               reads=(("BBB",),), writes=(("BBB16", ncj),), dma=True)
        op("gpsimd", lambda e: e.dma_start(
            out=WC16[:, :].rearrange("p (n d) -> p n d", d=D),
            in_=wc_t.rearrange("(n p) d -> p n d", p=128)),
           writes=tuple(("WC16", ncj) for ncj in range(NC)), dma=True)

        SUMR = CWSB
        emit_phaseW(wjobs[2:])
        gpair = {"i": 0}

        def round_S1(k):
            for c in range(CPT):
                g = k * CPT + c
                if k == 0:
                    plan.alias(("XAG", g), ("XPP", g))
                else:
                    buf = XCH[c % 4]
                    op("scalar", lambda e, o=buf[:, :],
                       s=x4[(k * TT + c * 128):(k * TT + (c + 1) * 128), :]:
                       e.dma_start(out=o, in_=s), writes=(("XCH", c % 4),),
                       dma=True, dsem=f"XS{c % 4}")
                    plan.alias(("XAG", g), ("XCH", c % 4))
            for c in range(CPT):
                g = k * CPT + c
                zb = ZCH[g % 2]
                xnb = XNCH[c % 2]
                op("vector", lambda e, g=g, zb=zb, xb=XCH[c % 4]:
                   e.tensor_scalar(zb[:, :], xb[:, :], MUT[:, g:g + 1],
                                   RPT[:, g:g + 1], ALU.subtract, ALU.mult),
                   reads=(("XAG", g), ("MUT",), ("RPT",)),
                   writes=(("ZCH", g % 2),))
                plan.alias(("ZAG", g), ("ZCH", g % 2))
                op("scalar", lambda e, g=g, xnb=xnb, xb=XCH[c % 4]:
                   e.activation(xnb[:, :], xb[:, :], AF.Identity,
                                bias=XBB[:, g:g + 1], scale=RT[:, g:g + 1]),
                   reads=(("XAG", g), ("VA",), ("RT",)),
                   writes=(("WCSF", c % 2),))
                plan.alias(("XNG", g), ("WCSF", c % 2))
                emit_transpose(
                    lambda dc, zb=zb: zb[:, dc * 128:(dc + 1) * 128],
                    lambda dc, g=g: ("ZAG", g),
                    lambda dc, c=c, k=k: ZT[k % 2][:, dc * TT + c * 128:
                                                   dc * TT + (c + 1) * 128],
                    lambda dc, c=c, k=k: ("ZTp", k % 2, dc, c),
                    DC)
                emit_transpose(
                    lambda dc, xnb=xnb: xnb[:, dc * 128:(dc + 1) * 128],
                    lambda dc, g=g: ("XNG", g),
                    lambda dc, c=c, k=k: XNT2[k % 2][:, dc * TMEGA + c * 128:
                                                     dc * TMEGA + (c + 1) * 128],
                    lambda dc, c=c, k=k: ("XNTp", k % 2, dc, c),
                    DC)

        def xnt_keys_of(k):
            return tuple(("XNTp", k % 2, dc, tc)
                         for dc in range(DC) for tc in range(NTC))

        def round_gate(k):
            XNT = XNT2[k % 2]
            CWM = CWM2[k % 2]
            xk = xnt_keys_of(k)
            def _gmm(e, XNT=XNT):
                last = None
                for tc in range(NTC):
                    for dc in range(DC):
                        last = e.matmul(
                            PS_C[:, tc * 8:(tc + 1) * 8],
                            XNT[:, dc * TMEGA + tc * 128:
                                dc * TMEGA + (tc + 1) * 128],
                            GW16[:, dc * 8:(dc + 1) * 8],
                            start=(dc == 0), stop=(dc == DC - 1))
                return last
            op("tensor", _gmm, reads=xk + (("GW16",),), writes=(("PSC", 0),))
            op("vector", lambda e: e.tensor_tensor(GT0[:, :], PS_C[:, 0:8 * NTC],
               GBR[:, :], ALU.add), reads=(("PSC", 0), ("GBR",)),
               writes=(("GT0",),) + tuple(("ETB", t2) for t2 in range(NTC)))
            gt3 = GT0[:, :].rearrange("p (t e) -> p t e", e=8)
            op("vector", lambda e: e.tensor_reduce(MX[:, :], gt3, AX.X, ALU.max),
               reads=(("GT0",),), writes=(("MX",),))
            op("vector", lambda e: e.tensor_scalar(NMX[:, :], MX[:, :], -1.0,
               None, ALU.mult), reads=(("MX",),), writes=(("NMX",),))
            for tc in range(NTC):
                op("scalar", lambda e, tc=tc: e.activation(
                    ETB[:, tc * 8:(tc + 1) * 8], GT0[:, tc * 8:(tc + 1) * 8],
                    AF.Exp, bias=NMX[:, tc:tc + 1]),
                   reads=(("GT0",), ("NMX",)), writes=(("ETB", tc),))
            et3 = ETB[:, :].rearrange("p (t e) -> p t e", e=8)
            etk = tuple(("ETB", tc) for tc in range(NTC))
            op("vector", lambda e: e.tensor_reduce(SMS[:, :], et3, AX.X, ALU.add),
               reads=etk, writes=(("SMS",),))
            op("vector", lambda e: e.reciprocal(SIV[:, :], SMS[:, :]),
               reads=(("SMS",),), writes=(("SIV",),))
            op("vector", lambda e: e.tensor_reduce(M1[:, :], et3, AX.X, ALU.max),
               reads=etk, writes=(("M1",),))
            for tc in range(NTC):
                esl = slice(tc * 8, (tc + 1) * 8)
                op("vector", lambda e, tc=tc, esl=esl: e.scalar_tensor_tensor(
                    PMT[:, :], ETB[:, esl], M1[:, tc:tc + 1], ETB[:, esl],
                    ALU.is_lt, ALU.mult),
                   reads=(("ETB", tc), ("M1",)), writes=(("PMT",),))
                op("vector", lambda e, tc=tc: e.tensor_reduce(
                    M2[:, tc:tc + 1], PMT[:, :], AX.X, ALU.max),
                   reads=(("PMT",),), writes=(("M2", tc),))
                op("vector", lambda e, tc=tc, esl=esl: e.scalar_tensor_tensor(
                    CWU[:, :], ETB[:, esl], M2[:, tc:tc + 1], ETB[:, esl],
                    ALU.is_ge, ALU.mult),
                   reads=(("ETB", tc), ("M2", tc)), writes=(("PMT",),))
                op("vector", lambda e, tc=tc, esl=esl, CWM=CWM: e.tensor_scalar(
                    CWM[:, esl], CWU[:, :], SIV[:, tc:tc + 1], None, ALU.mult),
                   reads=(("PMT",), ("SIV",)), writes=(("CWMc", k % 2, tc),))

        def round_cwtr(k):
            CWM = CWM2[k % 2]
            CWAUG = CWAUG2[k % 2]
            for tc in range(NTC):
                esl = slice(tc * 8, (tc + 1) * 8)
                psb = PS_A if tc % 2 == 0 else PS_B
                pkk = ("PSA", 0) if tc % 2 == 0 else ("PSB", 0)
                op("tensor", lambda e, esl=esl, CWM=CWM, psb=psb: e.transpose(
                    psb[0:8, 128:256], CWM[:, esl], ID32[:, :]),
                   reads=(("CWMc", k % 2, tc), ("ID32",)), writes=(pkk,))
                cwsb = CWSB2[tc % 2]
                op("scalar", lambda e, psb=psb, cwsb=cwsb: e.activation(
                    cwsb[:, :], psb[0:8, 128:256], AF.Copy),
                   reads=(pkk,), writes=(("CWSB", 0),))
                op("gpsimd", lambda e, tc=tc, CWAUG=CWAUG, cwsb=cwsb:
                   e.tensor_copy(CWAUG[0:8, tc * 128:(tc + 1) * 128],
                                 cwsb[:, :]),
                   reads=(("CWSB", 0),), writes=(("CWAUGc", tc),))

        def expert_ffn(k, ex, shared):
            XNT = XNT2[k % 2]
            CWM = CWM2[k % 2]
            CWAUG = CWAUG2[k % 2]
            xk = xnt_keys_of(k)
            sws = SWS
            def load_pair(p, ex=ex):
                gp = gpair["i"]
                gpair["i"] += 1
                buf = W1P[gp % 3]
                pk = ("W1P", gp % 3)
                g0 = (ex * 3 * FC + 2 * p) * 128
                op("sync", lambda e,
                   o=buf[:, :].rearrange("r (h d) -> r h d", d=D),
                   s=wgt16[g0:g0 + 256, :].rearrange("(h r) d -> r h d", r=128):
                   e.dma_start(out=o, in_=s),
                   reads=(("WG", ex, 0), ("WG", ex, 1)),
                   writes=(pk,), dma=True, dsem=f"WP{gp % 3}")
                plan.alias(("W1Gp", k, ex, p), pk)
            load_pair(0)
            load_pair(1)
            for dh in range(2):
                g2 = (ex * 3 * FC + 2 * FC) * 128
                op("sync", lambda e, dh=dh,
                   o=W2E2[dh][:, :].rearrange("r (f d) -> r f d", d=DH),
                   s=wgt16[g2:g2 + FC * 128,
                           dh * DH:(dh + 1) * DH].rearrange(
                       "(f r) d -> r f d", r=128):
                   e.dma_start(out=o, in_=s),
                   reads=(("WG2", ex),),
                   writes=tuple(("W2E", dh, j) for j in range(FC)),
                   dma=True, dsem=f"WE{dh}")
            for p in range(FC):
                wb = W1P[(gpair["i"] - FC + p - 2 + FC) % 3] if False else None
                for th in range(NTH):
                    pa = PS_A[:, th * THW:(th + 1) * THW]
                    pb = PS_B[:, th * THW:(th + 1) * THW]
                    ka = ("PSA", th % 2)
                    kb = ("PSB", th % 2)
                    wbuf = W1P[(gpair["i"] - (FC - p)) % 3] if False else None
                    def _m1a(e, p=p, th=th, pa=pa, XNT=XNT, ex=ex, k=k):
                        wb_ = w1bufs[(k, ex, p)]
                        last = None
                        for dc in range(DC):
                            last = e.matmul(
                                pa, wb_[:, dc * 128:(dc + 1) * 128],
                                XNT[:, dc * TMEGA + th * THW:
                                    dc * TMEGA + (th + 1) * THW],
                                start=(dc == 0), stop=(dc == DC - 1))
                        return last
                    op("tensor", _m1a,
                       reads=xk + (("W1Gp", k, ex, p),), writes=(ka,))
                    def _m1b(e, p=p, th=th, pb=pb, XNT=XNT, ex=ex, k=k):
                        wb_ = w1bufs[(k, ex, p)]
                        last = None
                        for dc in range(DC):
                            last = e.matmul(
                                pb, wb_[:, D + dc * 128:D + (dc + 1) * 128],
                                XNT[:, dc * TMEGA + th * THW:
                                    dc * TMEGA + (th + 1) * THW],
                                start=(dc == 0), stop=(dc == DC - 1))
                        return last
                    op("tensor", _m1b,
                       reads=xk + (("W1Gp", k, ex, p),), writes=(kb,))
                    sb = SIL[th % 2]
                    op("scalar", lambda e, sb=sb, pa=pa, p=p, ex=ex:
                       e.activation(sb[:, :], pa, AF.Silu,
                                    bias=B1S[:, ex * 2 * FC + p:
                                             ex * 2 * FC + p + 1]),
                       reads=(ka, ("B1S",)), writes=(("SIL", th % 2),))
                    op("vector", lambda e, sb=sb, pb=pb, p=p, th=th, ex=ex:
                       e.scalar_tensor_tensor(
                           sws[:, p * TMEGA + th * THW:
                               p * TMEGA + (th + 1) * THW],
                           pb, B1S[:, ex * 2 * FC + FC + p:
                                   ex * 2 * FC + FC + p + 1],
                           sb[:, :], ALU.add, ALU.mult),
                       reads=(kb, ("SIL", th % 2), ("B1S",)),
                       writes=(("SWSp", p, th),))
                if p + 2 < FC:
                    load_pair(p + 2)
            sws_keys = tuple(("SWSp", p, th)
                             for p in range(FC) for th in range(NTH))
            hm_keys = tuple(("HM", ncj, 0) for ncj in range(NC))
            for dh in range(2):
                for tc in range(NTC):
                    bk = tc % 2
                    pc = PS_C[:, bk * PCW:bk * PCW + DH]
                    kc = ("PSC", bk)
                    if shared:
                        def _ymm(e, tc=tc, dh=dh, pc=pc, bk=bk, XNT=XNT,
                                 CWAUG=CWAUG):
                            last = None
                            for ncj in range(NC):
                                last = e.matmul(
                                    pc,
                                    HMEGA[:, ncj * TMEGA + tc * 128:
                                          ncj * TMEGA + (tc + 1) * 128],
                                    WC16[:, ncj * D + dh * DH:
                                         ncj * D + (dh + 1) * DH],
                                    start=(ncj == 0), stop=False)
                            last = e.matmul(
                                pc, CWAUG[0:9, tc * 128:(tc + 1) * 128],
                                RB2A16[0:9, dh * DH:(dh + 1) * DH],
                                start=False, stop=False)
                            for q in range(NDH):
                                dc = dh * NDH + q
                                last = e.matmul(
                                    PS_C[:, bk * PCW + q * 128:
                                         bk * PCW + (q + 1) * 128],
                                    XNT[:, dc * TMEGA + tc * 128:
                                        dc * TMEGA + (tc + 1) * 128],
                                    DIAG16[:, dc * 128:(dc + 1) * 128],
                                    start=False, stop=False)
                            for p in range(FC):
                                last = e.matmul(
                                    pc,
                                    sws[:, p * TMEGA + tc * 128:
                                        p * TMEGA + (tc + 1) * 128],
                                    W2E2[dh][:, p * DH:(p + 1) * DH],
                                    start=False, stop=(p == FC - 1))
                            return last
                        op("tensor", _ymm,
                           reads=hm_keys + sws_keys + xk
                           + tuple(("WC16", j) for j in range(NC))
                           + tuple(("W2E", dh, p) for p in range(FC))
                           + tuple(("CWAUGc", t2) for t2 in range(NTC))
                           + tuple(("DIAG16", dh * NDH + q)
                                   for q in range(NDH))
                           + (("CWAUG", 8), ("RB2A", 0), ("RB2A", 8)),
                           writes=(kc,))
                        op("vector", lambda e, tc=tc, dh=dh, pc=pc:
                           e.tensor_tensor(
                               OUTA[:, tc * D + dh * DH:tc * D + (dh + 1) * DH],
                               OUTA[:, tc * D + dh * DH:tc * D + (dh + 1) * DH],
                               pc, ALU.add),
                           reads=(kc, ("OUTA", tc)), writes=(("OUTA", tc),))
                    else:
                        def _m2(e, tc=tc, dh=dh, pc=pc):
                            last = None
                            for p in range(FC):
                                last = e.matmul(
                                    pc,
                                    sws[:, p * TMEGA + tc * 128:
                                        p * TMEGA + (tc + 1) * 128],
                                    W2E2[dh][:, p * DH:(p + 1) * DH],
                                    start=(p == 0), stop=(p == FC - 1))
                            return last
                        op("tensor", _m2,
                           reads=sws_keys
                           + tuple(("W2E", dh, p) for p in range(FC)),
                           writes=(kc,))
                        op("vector", lambda e, tc=tc, dh=dh, pc=pc, ex=ex,
                           CWM=CWM:
                           e.scalar_tensor_tensor(
                               OUTA[:, tc * D + dh * DH:tc * D + (dh + 1) * DH],
                               pc, CWM[:, tc * 8 + ex - 1:tc * 8 + ex],
                               OUTA[:, tc * D + dh * DH:tc * D + (dh + 1) * DH],
                               ALU.mult, ALU.add),
                           reads=(kc, ("CWMc", k % 2, tc), ("OUTA", tc)),
                           writes=(("OUTA", tc),))

        def round_outa_init(k, eng="gpsimd"):
            t0 = k * TMEGA
            op(eng, lambda e, t0=t0: e.dma_start(
                out=OUTA[:, :].rearrange("p (tc d) -> p tc d", d=D),
                in_=x4[t0:t0 + TMEGA, :].rearrange("(tc p) d -> p tc d", p=128)),
               writes=tuple(("OUTA", tc) for tc in range(NTC)), dma=True)

        def round_store(k):
            t0 = k * TMEGA
            op("gpsimd", lambda e, t0=t0: e.dma_start(
                out=y4[t0:t0 + TMEGA, :].rearrange("(tc p) d -> p tc d", p=128),
                in_=OUTA[:, :].rearrange("p (tc d) -> p tc d", d=D)),
               reads=tuple(("OUTA", tc) for tc in range(NTC)),
               writes=(("Y4", k),), dma=True)

        w1bufs = {}
        _gp = 0
        for _k in range(NTA):
            for _exi in range(NE + 1):
                _ex = _exi + 1 if _exi < NE else 0
                for _p in range(FC):
                    w1bufs[(_k, _ex, _p)] = W1P[_gp % 3]
                    _gp += 1

        round_outa_init(0, eng="scalar")
        round_S1(0)
        for k in range(NTA):
            if k >= 1:
                for exi in range(NE):
                    expert_ffn(k - 1, exi + 1, False)
                round_cwtr(k - 1)
                carry_round(k - 1)
            round_gate(k)
            attn_round(k)
            if k >= 1:
                expert_ffn(k - 1, 0, True)
                round_store(k - 1)
                round_outa_init(k)
            round_pack(k)
            if k + 1 < NTA:
                round_S1(k + 1)
        for exi in range(NE):
            expert_ffn(NTA - 1, exi + 1, False)
        round_cwtr(NTA - 1)
        carry_round(NTA - 1)
        expert_ffn(NTA - 1, 0, True)
        round_store(NTA - 1)

        plan.raw_wait("sync", DSEM["sync"], plan.cnt.get(DSEM["sync"], 0))
        plan.raw_wait("gpsimd", DSEM["gpsimd"], plan.cnt.get(DSEM["gpsimd"], 0))

        with nc.Block() as block:
            @block.sync
            def _(e):
                plan.run("sync", e, sems)

            @block.scalar
            def _(e):
                plan.run("scalar", e, sems)

            @block.vector
            def _(e):
                plan.run("vector", e, sems)

            @block.tensor
            def _(e):
                plan.run("tensor", e, sems)

            @block.gpsimd
            def _(e):
                plan.run("gpsimd", e, sems)

    return nc


def _prep_inputs(cfg, core, full):
    DC, TT, NTA, FC = cfg["DC"], cfg["TT"], cfg["NTA"], cfg["FC"]
    D = DC * 128
    TOK = TT * NTA
    NTC = cfg["TMEGA"] // 128
    b, pos = divmod(core, 4)
    NTA = cfg["NTA"]
    TTILE = TOK // NTA
    x = np.ascontiguousarray(np.concatenate(
        [full["x"][b, (k * 4 + pos) * TTILE:(k * 4 + pos + 1) * TTILE, :]
         for k in range(NTA)], axis=0))

    def chunked(v):
        return np.ascontiguousarray(v.reshape(DC, 128).T)

    w1s, w2s = [], []
    for e in range(9):
        w1 = full["w1"][e]
        w2 = full["w2"][e]
        w1s.append(np.ascontiguousarray(
            w1.reshape(DC, 128, 2 * FC, 128).transpose(2, 1, 0, 3)
            .reshape(2 * FC * 128, D)))
        w2s.append(np.ascontiguousarray(w2.reshape(FC * 128, D)))
    b1s = np.zeros((128, 9 * 2 * FC), np.float32)
    for e in range(9):
        b1s[:, e * 2 * FC:(e + 1) * 2 * FC] = \
            full["b1"][e].reshape(2 * FC, 128).T

    m8 = np.zeros((1, 8), np.float32)
    mb8 = np.zeros((1, 8), np.float32)
    for j in range(8):
        jb, jp = divmod(j, 4)
        if jb == b:
            mb8[0, j] = 1.0
            if jp < pos:
                m8[0, j] = 1.0
    m8r = np.tile(m8, (128, 1))
    mb8r = np.tile(mb8, (128, 1))

    import ml_dtypes
    return dict(
        x4=x,
        wa_t=np.ascontiguousarray(full["W_alpha_w"].T),
        wb_t=np.ascontiguousarray(full["W_B_w"].T),
        wc_t=np.ascontiguousarray(full["W_C_w"].T),
        gw_in=np.ascontiguousarray(
            full["gate_w"].reshape(DC, 128, 8).transpose(1, 0, 2)
            .reshape(128, DC * 8)),
        gb8_in=np.ascontiguousarray(np.tile(full["gate_b"][None, :], (1, NTC))),
        w1_all=np.concatenate(w1s, 0),
        w2_all=np.concatenate(w2s, 0),
        b1s_in=b1s,
        rb2_in=np.ascontiguousarray(full["rb2"]),
        wcb_in=full["W_C_b"][None, :].copy(),
        sb2_in=full["sb2"][None, :].copy(),
        alb_in=chunked(full["W_alpha_b"]),
        bbb_in=chunked(full["W_B_b"]),
        lnw_in=chunked(full["ln_w"]),
        lnb_in=chunked(full["ln_b"]),
        dpar_in=chunked(full["D_param"]),
        m8r_in=np.ascontiguousarray(m8r),
        mc8r_in=np.ascontiguousarray(1.0 - m8r),
        mb8r_in=np.ascontiguousarray(mb8r),
        mbc8r_in=np.ascontiguousarray(1.0 - mb8r),
        mi_in=np.tile(np.repeat(m8r, 2, axis=1),
                      (1, DC)).astype(ml_dtypes.bfloat16),
        mic_in=(np.tile(np.repeat(1.0 - m8r, 2, axis=1)
                        * np.tile(np.array([1.0, 0.0], np.float32), 8),
                        (1, DC))).astype(ml_dtypes.bfloat16),
        mbi_in=np.tile(np.repeat(mb8r, 2, axis=1),
                       (1, DC)).astype(ml_dtypes.bfloat16),
        mbic_in=(np.tile(np.repeat(1.0 - mb8r, 2, axis=1)
                         * np.tile(np.array([1.0, 0.0], np.float32), 8),
                         (1, DC))).astype(ml_dtypes.bfloat16),
        id16_in=np.eye(128, dtype=ml_dtypes.bfloat16),
        id32_in=np.eye(128, dtype=np.float32),
    )


def kernel(L0_attn_x, L0_attn_W_alpha_w, L0_attn_W_alpha_b, L0_attn_W_B_w,
           L0_attn_W_B_b, L0_attn_W_C_w, L0_attn_W_C_b, L0_attn_D_param,
           L0_attn_ln_w, L0_attn_ln_b, L0_ffn_gate_w, L0_ffn_gate_b,
           L0_ffn_sw1, L0_ffn_sb1, L0_ffn_sw2, L0_ffn_sb2, L0_ffn_rw1,
           L0_ffn_rb1, L0_ffn_rw2, L0_ffn_rb2):
    cfg = REAL_CFG
    D = cfg["DC"] * 128
    TOK = cfg["TT"] * cfg["NTA"]
    x = np.asarray(L0_attn_x, np.float32)
    full = dict(
        x=x,
        W_alpha_w=np.asarray(L0_attn_W_alpha_w, np.float32),
        W_alpha_b=np.asarray(L0_attn_W_alpha_b, np.float32),
        W_B_w=np.asarray(L0_attn_W_B_w, np.float32),
        W_B_b=np.asarray(L0_attn_W_B_b, np.float32),
        W_C_w=np.asarray(L0_attn_W_C_w, np.float32),
        W_C_b=np.asarray(L0_attn_W_C_b, np.float32),
        D_param=np.asarray(L0_attn_D_param, np.float32),
        ln_w=np.asarray(L0_attn_ln_w, np.float32),
        ln_b=np.asarray(L0_attn_ln_b, np.float32),
        gate_w=np.asarray(L0_ffn_gate_w, np.float32),
        gate_b=np.asarray(L0_ffn_gate_b, np.float32),
        w1=[np.asarray(L0_ffn_sw1[0], np.float32)]
           + [np.asarray(L0_ffn_rw1[e], np.float32) for e in range(8)],
        w2=[np.asarray(L0_ffn_sw2[0], np.float32)]
           + [np.asarray(L0_ffn_rw2[e], np.float32) for e in range(8)],
        b1=[np.asarray(L0_ffn_sb1[0], np.float32)]
           + [np.asarray(L0_ffn_rb1[e], np.float32) for e in range(8)],
        rb2=np.asarray(L0_ffn_rb2, np.float32),
        sb2=np.asarray(L0_ffn_sb2[0], np.float32),
    )
    nc = build_nc(cfg)
    in_maps = [_prep_inputs(cfg, c, full) for c in range(8)]
    res = run_bass_kernel_spmd(nc, in_maps, core_ids=list(range(8)))
    B, T = x.shape[0], x.shape[1]
    out = np.empty((B, T, D), np.float32)
    NTA = cfg["NTA"]
    TTILE = TOK // NTA
    for c in range(8):
        b, pos = divmod(c, 4)
        y = res.results[c]["y4"]
        for k in range(NTA):
            out[b, (k * 4 + pos) * TTILE:(k * 4 + pos + 1) * TTILE, :] = \
                y[k * TTILE:(k + 1) * TTILE, :]
    return out


# version 21
# speedup vs baseline: 1.0630x; 1.0242x over previous
import contextlib
import numpy as np

import concourse.bass as bass
import concourse.mybir as mybir
from concourse.bass_utils import run_bass_kernel_spmd

F32 = mybir.dt.float32
BF16 = mybir.dt.bfloat16
AF = mybir.ActivationFunctionType
ALU = mybir.AluOpType
AX = mybir.AxisListType

ENGINES = ("sync", "scalar", "vector", "tensor", "gpsimd")
PSEM = {e: "P_" + e for e in ENGINES}
DSEM = {e: "D_" + e for e in ENGINES}

REAL_CFG = dict(DC=8, TT=512, NTA=8, TMEGA=512, NMEGA=8, FC=8)


class Plan:

    def __init__(self):
        self.q = {e: [] for e in ENGINES}
        self.cnt = {}
        self.clock = {e: {} for e in ENGINES}
        self.buf = {}

    def _wait(self, eng, sem, val):
        if val <= 0 or self.clock[eng].get(sem, 0) >= val:
            return
        self.clock[eng][sem] = val
        self.q[eng].append(("w", sem, val))

    def alias(self, new_key, phys_key):
        self.buf[new_key] = self.buf[phys_key]

    def op(self, eng, fn, reads=(), writes=(), dma=False, cc=False,
           dsem=None):
        if cc:
            sem, amt = "CC", None
        elif dma:
            if dsem is not None:
                sem, amt = dsem, 16
            else:
                sem, amt = DSEM[eng], 16
                self._wait(eng, sem, self.cnt.get(sem, 0))
        else:
            sem, amt = PSEM[eng], 1
        for k in reads:
            st = self.buf.get(k)
            if st is None:
                continue
            for s, v in st["w"].items():
                if eng == "tensor" and s == PSEM["tensor"]:
                    continue
                self._wait(eng, s, v)
        for k in writes:
            st = self.buf.get(k)
            if st is not None:
                for s, v in st["w"].items():
                    if eng == "tensor" and s == PSEM["tensor"]:
                        continue
                    self._wait(eng, s, v)
                for s, v in st["r"].items():
                    if eng == "tensor" and s == PSEM["tensor"]:
                        continue
                    self._wait(eng, s, v)
        newv = self.cnt.get(sem, 0) + (1 if cc else (amt or 1))
        self.cnt[sem] = newv
        self.q[eng].append(("o", fn, sem, (None if cc else amt)))
        for k in writes:
            self.buf[k] = {"w": {sem: newv}, "r": {}}
        for k in reads:
            st = self.buf.get(k)
            if st is None:
                st = self.buf[k] = {"w": {}, "r": {}}
            st["r"][sem] = max(st["r"].get(sem, 0), newv)
        return newv

    def raw_wait(self, eng, sem, val):
        self._wait(eng, sem, val)

    def run(self, eng, handle, sems):
        for item in self.q[eng]:
            if item[0] == "w":
                handle.wait_ge(sems[item[1]], item[2])
            else:
                _, fn, sem, amt = item
                ins = fn(handle)
                if amt is None:
                    ins.then_inc(sems[sem])
                else:
                    ins.then_inc(sems[sem], amt)


def build_nc(cfg):
    DC = cfg["DC"]
    D = DC * 128
    NC = DC
    TT = cfg["TT"]
    CPT = TT // 128
    NTA = cfg["NTA"]
    TOK = TT * NTA
    TMEGA = cfg["TMEGA"]
    NTT = TMEGA // TT
    NTC = TMEGA // 128
    NMEGA = cfg["NMEGA"]
    FC = cfg["FC"]
    NE = 8
    DH = D // 2
    NDH = DH // 128
    THW = min(TMEGA, 512)
    NTH = TMEGA // THW
    EPS = 1e-5
    assert TT <= 512 and DH <= 512 and THW <= 512 and TOK == NMEGA * TMEGA

    nc = bass.Bass("TRN2", target_bir_lowering=False)

    def I(name, shape, dt=F32):
        return nc.dram_tensor(name, list(shape), dt, kind="ExternalInput").ap()

    x4 = I("x4", (TOK, D))
    wa_t = I("wa_t", (D, D))
    wb_t = I("wb_t", (D, D))
    wc_t = I("wc_t", (D, D))
    gw_in = I("gw_in", (128, DC * 8))
    gb8_in = I("gb8_in", (1, 8 * NTC))
    w1_all = I("w1_all", ((NE + 1) * 2 * FC * 128, D))
    w2_all = I("w2_all", ((NE + 1) * FC * 128, D))
    b1s_in = I("b1s_in", (128, (NE + 1) * 2 * FC))
    rb2_in = I("rb2_in", (NE, D))
    wcb_in = I("wcb_in", (1, D))
    sb2_in = I("sb2_in", (1, D))
    alb_in = I("alb_in", (128, NC))
    bbb_in = I("bbb_in", (128, NC))
    lnw_in = I("lnw_in", (128, DC))
    lnb_in = I("lnb_in", (128, DC))
    dpar_in = I("dpar_in", (128, DC))
    m8r_in = I("m8r_in", (128, 8))
    mc8r_in = I("mc8r_in", (128, 8))
    mb8r_in = I("mb8r_in", (128, 8))
    mbc8r_in = I("mbc8r_in", (128, 8))
    mi_in = I("mi_in", (128, 16 * NC), BF16)
    mic_in = I("mic_in", (128, 16 * NC), BF16)
    mbi_in = I("mbi_in", (128, 16 * NC), BF16)
    mbic_in = I("mbic_in", (128, 16 * NC), BF16)
    id16_in = I("id16_in", (128, 128), BF16)
    id32_in = I("id32_in", (128, 128))

    y4 = nc.dram_tensor("y4", [TOK, D], F32, kind="ExternalOutput").ap()

    cc_in = nc.dram_tensor("cc_in", [2, D], F32)
    cc_out = nc.dram_tensor("cc_out", [16, D], F32)
    wgt16 = nc.dram_tensor("wgt16", [(NE + 1) * 3 * FC * 128, D], BF16)

    plan = Plan()

    with contextlib.ExitStack() as es:
        def SB(name, shape, dt):
            return es.enter_context(nc.sbuf_tensor(name, list(shape), dt))
        def PSUM(name, shape, dt):
            return es.enter_context(nc.psum_tensor(name, list(shape), dt))

        WA16 = SB("WA16", (128, DC * D), BF16)
        WB16 = SB("WB16", (128, DC * D), BF16)
        WC16 = SB("WC16", (128, NC * D), BF16)
        GW16 = SB("GW16", (128, DC * 8), BF16)
        GB8R = SB("GB8R", (1, 8 * NTC), F32)
        GBR = SB("GBR", (128, 8 * NTC), F32)
        DIAG16 = SB("DIAG16", (128, D), BF16)
        DRC = SB("DRC", (128, DC), F32)
        LNW = SB("LNW", (128, DC), F32)
        LNB = SB("LNB", (128, DC), F32)
        ALB = SB("ALB", (128, NC), F32)
        BBB = SB("BBB", (128, NC), F32)
        BBB16 = SB("BBB16", (1, D), BF16)
        B1S = SB("B1S", (128, (NE + 1) * 2 * FC), F32)
        RB2A16 = SB("RB2A16", (9, D), BF16)
        ONES16 = SB("ONES16", (1, max(TT, TMEGA)), BF16)
        ONE1 = SB("ONE1", (1, 128), F32)
        EPSC = SB("EPSC", (128, 1), F32)
        NHC = SB("NHC", (128, 1), F32)
        NE1C = SB("NE1C", (128, 1), F32)
        ID16 = SB("ID16", (128, 128), BF16)
        ID32 = SB("ID32", (128, 128), F32)
        MI = SB("MI", (128, 16 * NC), BF16)
        MIC = SB("MIC", (128, 16 * NC), BF16)
        MBI = SB("MBI", (128, 16 * NC), BF16)
        MBIC = SB("MBIC", (128, 16 * NC), BF16)
        FCAR = SB("FCAR", (128, 16 * NC), F32)
        FG = SB("FG", (128, 16 * NC), F32)
        CFO = SB("CFO", (128, 8 * NC), F32)
        GFO = [SB("GFO0", (128, 8 * NC), F32),
               SB("GFO1", (128, 8 * NC), F32)]
        ATOT = SB("ATOT", (128, NC), F32)
        HLAST = SB("HLAST", (128, NC), F32)
        NSC = TOK // 128
        S1T = SB("S1T", (128, NSC), F32)
        S2T = SB("S2T", (128, NSC), F32)
        MUT = SB("MUT", (128, NSC), F32)
        RT = SB("RT", (128, NSC), F32)
        RPT = SB("RPT", (128, NSC), F32)
        VT = SB("VT", (128, NSC), F32)
        VA = SB("VA", (128, NSC), F32)
        VB = SB("VB", (128, NSC), F32)
        ZBB = VT
        XBB = VA
        XCH = [SB(f"XCH{i}", (128, D), F32) for i in range(4)]
        ZCH = [SB("ZCH0", (128, D), BF16), SB("ZCH1", (128, D), BF16)]
        SQD = ZCH[0]
        CCB = XCH[0]
        ZT = [SB("ZT0", (128, DC * TT), BF16), SB("ZT1", (128, DC * TT), BF16)]
        ALS = [SB("ALS0", (128, TT), F32), SB("ALS1", (128, TT), F32)]
        HS = [SB("HS0", (128, TT), F32), SB("HS1", (128, TT), F32)]
        WCS = [SB("WCS0", (128, D), BF16), SB("WCS1", (128, D), BF16)]
        XNCH = WCS
        BT0 = XCH[0]
        BT1 = XCH[1]
        OUTA = SB("OUTA", (128, NTC * D), F32)
        NWST = 2 if NTC >= 2 else 1
        WST = [OUTA[:, 0:D], OUTA[:, D:2 * D] if NTC >= 2 else OUTA[:, 0:D]]
        XNT2 = [SB("XNT0", (128, DC * TMEGA), BF16),
                SB("XNT1", (128, DC * TMEGA), BF16)]
        HMEGA = SB("HMEGA", (128, NC * TMEGA), BF16)
        HLOC = SB("HLOC", (128, NC * TMEGA), BF16)
        PLOC = SB("PLOC", (128, NC * TMEGA), BF16)

        SWS = SB("SWS", (128, FC * TMEGA), BF16)
        W1P = [SB(f"W1P{i}", (128, 2 * D), BF16) for i in range(3)]
        W2E2 = [SB("W2E0", (128, FC * DH), BF16),
                SB("W2E1", (128, FC * DH), BF16)]
        SIL = [SB("SIL0", (128, THW), F32), SB("SIL1", (128, THW), F32)]
        GT0 = SB("GT0", (128, 8 * NTC), F32)
        ETB = GT0
        MX = SB("MX", (128, NTC), F32)
        NMX = SB("NMX", (128, NTC), F32)
        SMS = SB("SMS", (128, NTC), F32)
        SIV = SB("SIV", (128, NTC), F32)
        M1 = SB("M1", (128, NTC), F32)
        M2 = SB("M2", (128, NTC), F32)
        PMT = SB("PMT", (128, 8), F32)
        CWU = PMT
        CWM2 = [SB("CWM0", (128, 8 * NTC), F32),
                SB("CWM1", (128, 8 * NTC), F32)]
        CWAUG1T = SB("CWAUG", (9, TMEGA), BF16)
        CWAUG2 = [CWAUG1T, CWAUG1T]
        CWSB = SB("CWSB", (8, 128), F32)
        CWSB2 = [CWSB, CWSB]
        AJH = SB("AJH", (128, 16 * NC), F32)

        PSW = max(TMEGA, D, 512)
        PS_A = PSUM("PS_A", (128, PSW), F32)
        PS_B = PSUM("PS_B", (128, PSW), F32)
        PS_C = PSUM("PS_C", (128, PSW), F32)
        PS_T1 = PSUM("PS_T1", (128, 1024), BF16)
        PS_T2 = PSUM("PS_T2", (128, 1024), BF16)
        PCW = PSW // 2

        sem_names = ([PSEM[e] for e in ENGINES] + [DSEM[e] for e in ENGINES]
                     + ["CC"] + [f"XS{i}" for i in range(4)]
                     + ["WP0", "WP1", "WP2", "WE0", "WE1", "WS0", "WS1"])
        sems = {n: es.enter_context(nc.semaphore(n)) for n in sem_names}

        op = plan.op

        def stats_chain(ncols):
            sl = slice(0, ncols)
            rs1 = [("S1T", c) for c in range(ncols)]
            rs2 = [("S2T", c) for c in range(ncols)]
            op("vector", lambda e: e.tensor_scalar(MUT[:, sl], S1T[:, sl],
               1.0 / D, None, ALU.mult), reads=rs1, writes=(("MUT",),))
            op("vector", lambda e: e.tensor_scalar(VA[:, sl], S2T[:, sl],
               1.0 / D, None, ALU.mult), reads=rs2, writes=(("VA",),))
            op("vector", lambda e: e.tensor_tensor(VB[:, sl], MUT[:, sl],
               MUT[:, sl], ALU.mult), reads=(("MUT",),), writes=(("VB",),))
            op("vector", lambda e: e.tensor_tensor(VT[:, sl], VA[:, sl],
               VB[:, sl], ALU.subtract),
               reads=(("VA",), ("VB",)), writes=(("VT",),))
            op("scalar", lambda e: e.activation(VA[:, sl], VT[:, sl], AF.Ln,
               bias=EPSC[:, 0:1]), reads=(("VT",), ("EPSC",)), writes=(("VA",),))
            op("scalar", lambda e: e.activation(RT[:, sl], VA[:, sl], AF.Exp,
               scale=NHC[:, 0:1]), reads=(("VA",), ("NHC",)), writes=(("RT",),))
            op("scalar", lambda e: e.activation(VB[:, sl], VA[:, sl], AF.Exp,
               scale=NE1C[:, 0:1]), reads=(("VA",), ("NE1C",)), writes=(("VB",),))
            op("vector", lambda e: e.tensor_tensor(VB[:, sl], VT[:, sl],
               VB[:, sl], ALU.mult), reads=(("VT",), ("VB",)), writes=(("VB",),))
            op("scalar", lambda e: e.activation(VB[:, sl], VB[:, sl], AF.Ln,
               bias=EPSC[:, 0:1]), reads=(("VB",), ("EPSC",)), writes=(("VB",),))
            op("scalar", lambda e: e.activation(VB[:, sl], VB[:, sl], AF.Exp,
               scale=NHC[:, 0:1]), reads=(("VB",), ("NHC",)), writes=(("VB",),))
            op("vector", lambda e: e.tensor_tensor(RPT[:, sl], RT[:, sl],
               VB[:, sl], ALU.mult), reads=(("RT",), ("VB",)), writes=(("RPT",),))
            op("vector", lambda e: e.tensor_tensor(ZBB[:, sl], MUT[:, sl],
               RPT[:, sl], ALU.mult), reads=(("MUT",), ("RPT",)),
               writes=(("VT",),))
            op("vector", lambda e: e.tensor_scalar(ZBB[:, sl], ZBB[:, sl],
               -1.0, None, ALU.mult), reads=(("ZBB",),), writes=(("VT",),))
            op("vector", lambda e: e.tensor_tensor(XBB[:, sl], MUT[:, sl],
               RT[:, sl], ALU.mult), reads=(("MUT",), ("RT",)),
               writes=(("VA",),))
            op("vector", lambda e: e.tensor_scalar(XBB[:, sl], XBB[:, sl],
               -1.0, None, ALU.mult), reads=(("XBB",),), writes=(("VA",),))

        tr_state = {"slot": 0}
        def emit_transpose(src_ap_fn, src_key_fn, dst_ap_fn, dst_key_fn, n):
            for i in range(n):
                slot = tr_state["slot"]
                tr_state["slot"] = slot + 1
                psb = PS_T1 if slot % 2 == 0 else PS_T2
                pkey = ("PSTR", slot % 2)
                pap = psb[:, 0:128]
                op("tensor", lambda e, pap=pap, s=src_ap_fn(i):
                   e.transpose(pap, s, ID16[:, :]),
                   reads=(src_key_fn(i), ("ID16",)), writes=(pkey,))
                op("scalar", lambda e, pap=pap, d=dst_ap_fn(i):
                   e.activation(d, pap, AF.Copy),
                   reads=(pkey,), writes=(dst_key_fn(i),))

        def attn_round(k):
            zt = ZT[k % 2]
            ztile_keys = tuple(("ZTp", k % 2, dc, c)
                               for dc in range(DC) for c in range(CPT))
            for ncj in range(NC):
                pa = PS_A[:, (ncj % 2) * TT:((ncj % 2) + 1) * TT]
                pu = PS_B[:, (ncj % 2) * TT:((ncj % 2) + 1) * TT]
                ka = ("PSA", ncj % 2)
                ku = ("PSB", ncj % 2)
                def _amm(e, ncj=ncj, pa=pa, zt=zt):
                    last = None
                    for dc in range(DC):
                        last = e.matmul(pa, WA16[:, dc * D + ncj * 128:
                                                 dc * D + (ncj + 1) * 128],
                                        zt[:, dc * TT:(dc + 1) * TT],
                                        start=(dc == 0), stop=(dc == DC - 1))
                    return last
                op("tensor", _amm,
                   reads=ztile_keys + tuple(("W16", 0, dc) for dc in range(DC)),
                   writes=(ka,))
                def _umm(e, ncj=ncj, pu=pu, zt=zt):
                    for dc in range(DC):
                        e.matmul(pu, WB16[:, dc * D + ncj * 128:
                                          dc * D + (ncj + 1) * 128],
                                 zt[:, dc * TT:(dc + 1) * TT],
                                 start=(dc == 0), stop=False)
                    return e.matmul(pu, BBB16[0:1, ncj * 128:(ncj + 1) * 128],
                                    ONES16[0:1, 0:TT], start=False, stop=True)
                op("tensor", _umm,
                   reads=ztile_keys + tuple(("W16", 1, dc) for dc in range(DC))
                   + (("BBB16", ncj), ("ONES16",)),
                   writes=(ku,))
                ab = ALS[ncj % 2]
                op("scalar", lambda e, ab=ab, pa=pa, ncj=ncj: e.activation(
                    ab[:, :], pa, AF.Sigmoid, bias=ALB[:, ncj:ncj + 1]),
                   reads=(ka, ("ALB",)), writes=(("ALS", ncj % 2),))
                hb = HS[ncj % 2]
                op("vector", lambda e, hb=hb, ab=ab, pu=pu:
                   e.tensor_tensor_scan(hb[:, :], ab[:, :], pu, 0.0,
                                        ALU.mult, ALU.add),
                   reads=(("ALS", ncj % 2), ku), writes=(("HS", ncj % 2),))
                op("vector", lambda e, ncj=ncj, ab=ab:
                   e.tensor_tensor_scan(
                       PLOC[:, ncj * TMEGA:(ncj + 1) * TMEGA],
                       ab[:, :], ab[:, :], 1.0, ALU.mult, ALU.bypass),
                   reads=(("ALS", ncj % 2),), writes=(("PLOC", ncj),))
                op("vector", lambda e, hb=hb, ncj=ncj: e.tensor_copy(
                    HLOC[:, ncj * TMEGA:(ncj + 1) * TMEGA], hb[:, :]),
                   reads=(("HS", ncj % 2),), writes=(("HLOC", ncj),))
                op("vector", lambda e, hb=hb, ncj=ncj: e.tensor_copy(
                    HLAST[:, ncj:ncj + 1], hb[:, TT - 1:TT]),
                   reads=(("HS", ncj % 2),), writes=(("HLAST", ncj),))
                op("vector", lambda e, ab=ab, ncj=ncj: e.tensor_reduce(
                    ATOT[:, ncj:ncj + 1], ab[:, :], AX.X, ALU.mult),
                   reads=(("ALS", ncj % 2),), writes=(("ATOT", ncj),))
        def round_pack(k):
            for which, tile, keys in (
                    (0, ATOT, tuple(("ATOT", j) for j in range(NC))),
                    (1, HLAST, tuple(("HLAST", j) for j in range(NC)))):
                op("tensor", lambda e, t=tile: e.transpose(
                    PS_C[0:NC, 0:128], t[:, :], ID32[:, :]),
                   reads=keys + (("ID32",),), writes=(("PSC", 0),))
                op("scalar", lambda e: e.activation(SUMR[0:NC, :],
                   PS_C[0:NC, 0:128], AF.Copy),
                   reads=(("PSC", 0),), writes=(("CWSB", 0),))
                op("scalar", lambda e, w=which: e.dma_start(
                    out=cc_in[w:w + 1, :].rearrange("o (c p) -> o c p", p=128),
                    in_=SUMR[0:NC, :]),
                   reads=(("CWSB", 0),), writes=(("CCI", which),), dma=True)
            op("gpsimd", lambda e: e.collective_compute(
                "AllGather", ALU.bypass, replica_groups=[list(range(8))],
                ins=[cc_in.ap().opt()], outs=[cc_out.ap().opt()]),
               reads=(("CCI", 0), ("CCI", 1)), writes=(("CCO",),), cc=True)

        def carry_round(k):
            op("scalar", lambda e: e.dma_start(out=CCB[0:16, :],
               in_=cc_out[:, :]),
               reads=(("CCO",),), writes=(("XCH", 0),), dma=True)
            plan.alias(("CCB",), ("XCH", 0))
            for c in range(NC):
                psb = PS_A if c % 2 == 0 else PS_B
                pkk = ("PSA", 0) if c % 2 == 0 else ("PSB", 0)
                op("tensor", lambda e, c=c, psb=psb: e.transpose(
                    psb[0:128, 0:16], CCB[0:16, c * 128:(c + 1) * 128],
                    ID32[0:16, 0:16]),
                   reads=(("CCB",), ("ID32",)), writes=(pkk,))
                op("scalar", lambda e, c=c, psb=psb: e.activation(
                    AJH[:, c * 16:(c + 1) * 16], psb[0:128, 0:16],
                    AF.Copy),
                   reads=(pkk,), writes=(("AJH", c),))
            ajhk = tuple(("AJH", c) for c in range(NC))
            gs, gd = (k + 1) % 2, k % 2
            for (dst, dkey, mm, mmc) in ((FCAR, ("FCAR",), MI, MIC),
                                         (FG, ("FG",), MBI, MBIC)):
                op("vector", lambda e, dst=dst, mm=mm: e.tensor_tensor(
                    dst[:, :], AJH[:, :], mm[:, :], ALU.mult),
                   reads=ajhk + (("MI",), ("MBI",)), writes=(dkey,))
                op("vector", lambda e, dst=dst, mmc=mmc: e.tensor_tensor(
                    dst[:, :], dst[:, :], mmc[:, :], ALU.add),
                   reads=(dkey, ("MIC",), ("MBIC",)), writes=(dkey,))
            for c in range(NC):
                op("vector", lambda e, c=c, gs=gs: e.tensor_tensor_scan(
                    CFO[:, c * 8:(c + 1) * 8],
                    FCAR[:, c * 16:(c + 1) * 16].rearrange(
                        "p (j t) -> p j t", t=2)[:, :, 0],
                    FCAR[:, c * 16:(c + 1) * 16].rearrange(
                        "p (j t) -> p j t", t=2)[:, :, 1],
                    GFO[gs][:, c * 8 + 7:c * 8 + 8],
                    ALU.mult, ALU.add),
                   reads=(("FCAR",), ("GFO", gs, c)), writes=(("CFO", c),))
                op("vector", lambda e, c=c, gs=gs, gd=gd: e.tensor_tensor_scan(
                    GFO[gd][:, c * 8:(c + 1) * 8],
                    FG[:, c * 16:(c + 1) * 16].rearrange(
                        "p (j t) -> p j t", t=2)[:, :, 0],
                    FG[:, c * 16:(c + 1) * 16].rearrange(
                        "p (j t) -> p j t", t=2)[:, :, 1],
                    GFO[gs][:, c * 8 + 7:c * 8 + 8],
                    ALU.mult, ALU.add),
                   reads=(("FG",), ("GFO", gs, c)), writes=(("GFO", gd, c),))
            for ncj in range(NC):
                op("vector", lambda e, ncj=ncj: e.scalar_tensor_tensor(
                    HMEGA[:, ncj * TMEGA:(ncj + 1) * TMEGA],
                    PLOC[:, ncj * TMEGA:(ncj + 1) * TMEGA],
                    CFO[:, ncj * 8 + 7:ncj * 8 + 8],
                    HLOC[:, ncj * TMEGA:(ncj + 1) * TMEGA],
                    ALU.mult, ALU.add),
                   reads=(("PLOC", ncj), ("HLOC", ncj), ("CFO", ncj)),
                   writes=(("HM", ncj, 0),))

        wjobs = []
        for e_ in list(range(1, NE + 1)) + [0]:
            wjobs.append(("w1", e_))
            wjobs.append(("w2", e_))

        def emit_phaseW(jobs):
            for kind, e_ in jobs:
                if kind == "w1":
                    s0 = e_ * 2 * FC * 128
                    d0 = e_ * 3 * FC * 128
                    dblk = wgt16[d0:d0 + 2 * FC * 128, :].rearrange(
                        "(p q) d -> p q d", q=256)
                    for h in range(2):
                        src_ap = w1_all[s0 + h * FC * 128:
                                        s0 + (h + 1) * FC * 128, :].rearrange(
                            "(p r) d -> p r d", r=128)
                        dst_ap = dblk[:, h * 128:(h + 1) * 128, :]
                        op("gpsimd", lambda e, o=dst_ap, s=src_ap:
                           e.dma_start(out=o, in_=s),
                           writes=(("WG", e_, h),), dma=True)
                else:
                    s0 = e_ * FC * 128
                    src_ap = w2_all[s0:s0 + FC * 128, :]
                    d0 = (e_ * 3 * FC + 2 * FC) * 128
                    dst_ap = wgt16[d0:d0 + FC * 128, :]
                    op("gpsimd", lambda e, o=dst_ap, s=src_ap:
                       e.dma_start(out=o, in_=s),
                       writes=(("WG2", e_),), dma=True)

        def ld(dst_ap, src_ap, key, reads=()):
            op("sync", lambda e, o=dst_ap, i=src_ap: e.dma_start(out=o, in_=i),
               reads=reads, writes=(key,), dma=True)

        emit_phaseW(wjobs[:2])
        ld(LNW[:, :], lnw_in, ("LNW",))
        ld(LNB[:, :], lnb_in, ("LNB",))
        ld(DRC[:, :], dpar_in, ("DRC",))
        ld(MI[:, :], mi_in, ("MI",))
        ld(MIC[:, :], mic_in, ("MIC",))
        ld(MBI[:, :], mbi_in, ("MBI",))
        ld(MBIC[:, :], mbic_in, ("MBIC",))
        op("vector", lambda e: e.memset(GFO[1][:, :], 0.0),
           writes=tuple(("GFO", 1, c) for c in range(NC)))
        ld(ID16[:, :], id16_in, ("ID16",))
        ld(ID32[:, :], id32_in, ("ID32",))
        ld(B1S[:, :], b1s_in, ("B1S",))

        op("sync", lambda e: e.dma_start(out=OUTA[0:1, 0:D], in_=wcb_in),
           writes=(("OUTA", 0),), dma=True)
        plan.alias(("BT0",), ("OUTA", 0))

        ld(GB8R[:, :], gb8_in, ("GB8R",))
        ld(ALB[:, :], alb_in, ("ALB",))
        ld(BBB[:, :], bbb_in, ("BBB",))
        op("vector", lambda e: e.memset(ONES16[:, :], 1.0), writes=(("ONES16",),))
        op("vector", lambda e: e.memset(ONE1[:, :], 1.0), writes=(("ONE1",),))
        op("vector", lambda e: e.memset(EPSC[:, :], EPS), writes=(("EPSC",),))
        op("vector", lambda e: e.memset(NHC[:, :], -0.5), writes=(("NHC",),))
        op("vector", lambda e: e.memset(NE1C[:, :], -1.0), writes=(("NE1C",),))
        NSC = TOK // 128
        for gi in reversed(range(NSC)):
            r0 = gi * 128
            buf = XCH[gi % 4]
            op("scalar" if gi % 2 == 0 else "sync",
               lambda e, o=buf[:, :], s=x4[r0:r0 + 128, :]:
               e.dma_start(out=o, in_=s), writes=(("XCH", gi % 4),), dma=True,
               dsem=f"XS{gi % 4}")
            plan.alias(("XPP", gi), ("XCH", gi % 4))
            op("vector", lambda e, gi=gi, xb=buf: e.reduce_sum(
                S1T[:, gi:gi + 1], xb[:, :], AX.X),
               reads=(("XPP", gi),), writes=(("S1T", gi),))
            op("scalar", lambda e, gi=gi, xb=buf: e.activation(
                SQD[:, :], xb[:, :], AF.Square,
                accum_out=S2T[:, gi:gi + 1]),
               reads=(("XPP", gi),), writes=(("S2T", gi), ("ZCH", 0)))
        stats_chain(NSC)


        op("gpsimd", lambda e: e.dma_start(out=GW16[:, :], in_=gw_in),
           writes=(("GW16",),), dma=True)
        op("tensor", lambda e: e.matmul(PS_A[:, 0:8 * NTC], ONE1[0:1, :],
                                        GB8R[0:1, :], start=True, stop=True),
           reads=(("GB8R",), ("ONE1",)), writes=(("PSA", 0),))
        op("vector", lambda e: e.tensor_copy(GBR[:, :], PS_A[:, 0:8 * NTC]),
           reads=(("PSA", 0),), writes=(("GBR",),))
        for dc in range(DC):
            op("vector", lambda e, dc=dc: e.tensor_scalar(
                DIAG16[:, dc * 128:(dc + 1) * 128], ID16[:, :],
                DRC[:, dc:dc + 1], None, ALU.mult),
               reads=(("ID16",), ("DRC",)), writes=(("DIAG16", dc),))

        op("gpsimd", lambda e: e.dma_start(out=RB2A16[0:8, :], in_=rb2_in),
           writes=(("RB2A", 0),), dma=True)
        op("gpsimd", lambda e: e.dma_start(out=OUTA[0:1, 0:D], in_=sb2_in,
                                           accum_op=ALU.add),
           reads=(("BT0",),), writes=(("OUTA", 0),), dma=True)
        plan.alias(("BT0",), ("OUTA", 0))
        plan.alias(("BT0",), ("OUTA", 0))
        op("gpsimd", lambda e: e.dma_start(out=RB2A16[8:9, :],
                                           in_=OUTA[0:1, 0:D]),
           reads=(("BT0",),), writes=(("RB2A", 8),), dma=True)
        op("gpsimd", lambda e: e.dma_start(
            out=CWAUG1T[8:9, :], in_=ONES16[0:1, 0:TMEGA]),
           reads=(("ONES16",),), writes=(("CWAUG", 8),), dma=True)
        for wsel, (src, dst16) in enumerate([(wa_t, WA16), (wb_t, WB16)]):
            for dc in range(DC):
                buf = WST[dc % NWST]
                pk = ("OUTA", dc % NWST)
                op("sync", lambda e, o=buf,
                   i=src[dc * 128:(dc + 1) * 128, :]: e.dma_start(out=o, in_=i),
                   writes=(pk,), dma=True, dsem=f"WS{dc % NWST}")
                plan.alias(("WSTP", wsel, dc), pk)
                op("vector", lambda e, o=dst16[:, dc * D:(dc + 1) * D],
                   i=buf, s=LNW[:, dc:dc + 1]:
                   e.tensor_scalar(o, i, s, None, ALU.mult),
                   reads=(("WSTP", wsel, dc), ("LNW",)),
                   writes=(("W16", wsel, dc),))
                def _mv(e, dc=dc, wsel=wsel, buf=buf):
                    last = None
                    for ncj in range(NC):
                        col = PCW + wsel * NC + ncj
                        st = (wsel == 0 and dc == 0 and ncj == 0)
                        sp = (wsel == 1 and dc == DC - 1 and ncj == NC - 1)
                        last = e.matmul(PS_C[:, col:col + 1],
                                        buf[:, ncj * 128:(ncj + 1) * 128],
                                        LNB[:, dc:dc + 1],
                                        start=st, stop=sp)
                    return last
                buf = None
                op("tensor", _mv, reads=(("WSTP", wsel, dc), ("LNB",)),
                   writes=(("PSC", 1),))
        op("vector", lambda e: e.tensor_tensor(
            ALB[:, :], ALB[:, :], PS_C[:, PCW:PCW + NC], ALU.add),
           reads=(("ALB",), ("PSC", 1)), writes=(("ALB",),))
        op("vector", lambda e: e.tensor_tensor(
            BBB[:, :], BBB[:, :], PS_C[:, PCW + NC:PCW + 2 * NC], ALU.add),
           reads=(("BBB",), ("PSC", 1)), writes=(("BBB",),))
        for ncj in range(NC):
            op("gpsimd", lambda e, ncj=ncj: e.dma_start(
                out=BBB16[0:1, ncj * 128:(ncj + 1) * 128],
                in_=BBB[:, ncj:ncj + 1]),
               reads=(("BBB",),), writes=(("BBB16", ncj),), dma=True)
        op("gpsimd", lambda e: e.dma_start(
            out=WC16[:, :].rearrange("p (n d) -> p n d", d=D),
            in_=wc_t.rearrange("(n p) d -> p n d", p=128)),
           writes=tuple(("WC16", ncj) for ncj in range(NC)), dma=True)

        SUMR = CWSB
        emit_phaseW(wjobs[2:])
        gpair = {"i": 0}

        def round_S1a(k):
            for c in range(CPT):
                g = k * CPT + c
                if k == 0:
                    plan.alias(("XAG", g), ("XPP", g))
                else:
                    buf = XCH[c % 4]
                    op("scalar", lambda e, o=buf[:, :],
                       s=x4[(k * TT + c * 128):(k * TT + (c + 1) * 128), :]:
                       e.dma_start(out=o, in_=s), writes=(("XCH", c % 4),),
                       dma=True, dsem=f"XS{c % 4}")
                    plan.alias(("XAG", g), ("XCH", c % 4))
            for c in range(min(2, CPT)):
                _S1_zxn(k, c)

        def _S1_zxn(k, c):
            g = k * CPT + c
            zb = ZCH[g % 2]
            xnb = XNCH[c % 2]
            op("vector", lambda e, g=g, zb=zb, xb=XCH[c % 4]:
               e.tensor_scalar(zb[:, :], xb[:, :], MUT[:, g:g + 1],
                               RPT[:, g:g + 1], ALU.subtract, ALU.mult),
               reads=(("XAG", g), ("MUT",), ("RPT",)),
               writes=(("ZCH", g % 2),))
            plan.alias(("ZAG", g), ("ZCH", g % 2))
            op("scalar", lambda e, g=g, xnb=xnb, xb=XCH[c % 4]:
               e.activation(xnb[:, :], xb[:, :], AF.Identity,
                            bias=XBB[:, g:g + 1], scale=RT[:, g:g + 1]),
               reads=(("XAG", g), ("VA",), ("RT",)),
               writes=(("WCSF", c % 2),))
            plan.alias(("XNG", g), ("WCSF", c % 2))

        def round_S1T(k):
            for c in range(CPT):
                g = k * CPT + c
                zb = ZCH[g % 2]
                xnb = XNCH[c % 2]
                if c >= 2:
                    _S1_zxn(k, c)
                emit_transpose(
                    lambda dc, zb=zb: zb[:, dc * 128:(dc + 1) * 128],
                    lambda dc, g=g: ("ZAG", g),
                    lambda dc, c=c, k=k: ZT[k % 2][:, dc * TT + c * 128:
                                                   dc * TT + (c + 1) * 128],
                    lambda dc, c=c, k=k: ("ZTp", k % 2, dc, c),
                    DC)
                emit_transpose(
                    lambda dc, xnb=xnb: xnb[:, dc * 128:(dc + 1) * 128],
                    lambda dc, g=g: ("XNG", g),
                    lambda dc, c=c, k=k: XNT2[k % 2][:, dc * TMEGA + c * 128:
                                                     dc * TMEGA + (c + 1) * 128],
                    lambda dc, c=c, k=k: ("XNTp", k % 2, dc, c),
                    DC)

        def round_S1(k):
            round_S1a(k)
            round_S1T(k)

        def xnt_keys_of(k):
            return tuple(("XNTp", k % 2, dc, tc)
                         for dc in range(DC) for tc in range(NTC))

        def round_gate(k):
            XNT = XNT2[k % 2]
            CWM = CWM2[k % 2]
            xk = xnt_keys_of(k)
            def _gmm(e, XNT=XNT):
                last = None
                for tc in range(NTC):
                    for dc in range(DC):
                        last = e.matmul(
                            PS_C[:, tc * 8:(tc + 1) * 8],
                            XNT[:, dc * TMEGA + tc * 128:
                                dc * TMEGA + (tc + 1) * 128],
                            GW16[:, dc * 8:(dc + 1) * 8],
                            start=(dc == 0), stop=(dc == DC - 1))
                return last
            op("tensor", _gmm, reads=xk + (("GW16",),), writes=(("PSC", 0),))
            op("vector", lambda e: e.tensor_tensor(GT0[:, :], PS_C[:, 0:8 * NTC],
               GBR[:, :], ALU.add), reads=(("PSC", 0), ("GBR",)),
               writes=(("GT0",),) + tuple(("ETB", t2) for t2 in range(NTC)))
            gt3 = GT0[:, :].rearrange("p (t e) -> p t e", e=8)
            op("vector", lambda e: e.tensor_reduce(MX[:, :], gt3, AX.X, ALU.max),
               reads=(("GT0",),), writes=(("MX",),))
            op("vector", lambda e: e.tensor_scalar(NMX[:, :], MX[:, :], -1.0,
               None, ALU.mult), reads=(("MX",),), writes=(("NMX",),))
            for tc in range(NTC):
                op("scalar", lambda e, tc=tc: e.activation(
                    ETB[:, tc * 8:(tc + 1) * 8], GT0[:, tc * 8:(tc + 1) * 8],
                    AF.Exp, bias=NMX[:, tc:tc + 1]),
                   reads=(("GT0",), ("NMX",)), writes=(("ETB", tc),))
            et3 = ETB[:, :].rearrange("p (t e) -> p t e", e=8)
            etk = tuple(("ETB", tc) for tc in range(NTC))
            op("vector", lambda e: e.tensor_reduce(SMS[:, :], et3, AX.X, ALU.add),
               reads=etk, writes=(("SMS",),))
            op("vector", lambda e: e.reciprocal(SIV[:, :], SMS[:, :]),
               reads=(("SMS",),), writes=(("SIV",),))
            op("vector", lambda e: e.tensor_reduce(M1[:, :], et3, AX.X, ALU.max),
               reads=etk, writes=(("M1",),))
            for tc in range(NTC):
                esl = slice(tc * 8, (tc + 1) * 8)
                op("vector", lambda e, tc=tc, esl=esl: e.scalar_tensor_tensor(
                    PMT[:, :], ETB[:, esl], M1[:, tc:tc + 1], ETB[:, esl],
                    ALU.is_lt, ALU.mult),
                   reads=(("ETB", tc), ("M1",)), writes=(("PMT",),))
                op("vector", lambda e, tc=tc: e.tensor_reduce(
                    M2[:, tc:tc + 1], PMT[:, :], AX.X, ALU.max),
                   reads=(("PMT",),), writes=(("M2", tc),))
                op("vector", lambda e, tc=tc, esl=esl: e.scalar_tensor_tensor(
                    CWU[:, :], ETB[:, esl], M2[:, tc:tc + 1], ETB[:, esl],
                    ALU.is_ge, ALU.mult),
                   reads=(("ETB", tc), ("M2", tc)), writes=(("PMT",),))
                op("vector", lambda e, tc=tc, esl=esl, CWM=CWM: e.tensor_scalar(
                    CWM[:, esl], CWU[:, :], SIV[:, tc:tc + 1], None, ALU.mult),
                   reads=(("PMT",), ("SIV",)), writes=(("CWMc", k % 2, tc),))

        def round_cwtr(k):
            CWM = CWM2[k % 2]
            CWAUG = CWAUG2[k % 2]
            for tc in range(NTC):
                esl = slice(tc * 8, (tc + 1) * 8)
                psb = PS_A if tc % 2 == 0 else PS_B
                pkk = ("PSA", 0) if tc % 2 == 0 else ("PSB", 0)
                op("tensor", lambda e, esl=esl, CWM=CWM, psb=psb: e.transpose(
                    psb[0:8, 128:256], CWM[:, esl], ID32[:, :]),
                   reads=(("CWMc", k % 2, tc), ("ID32",)), writes=(pkk,))
                cwsb = CWSB2[tc % 2]
                op("scalar", lambda e, psb=psb, cwsb=cwsb: e.activation(
                    cwsb[:, :], psb[0:8, 128:256], AF.Copy),
                   reads=(pkk,), writes=(("CWSB", 0),))
                op("gpsimd", lambda e, tc=tc, CWAUG=CWAUG, cwsb=cwsb:
                   e.tensor_copy(CWAUG[0:8, tc * 128:(tc + 1) * 128],
                                 cwsb[:, :]),
                   reads=(("CWSB", 0),), writes=(("CWAUGc", tc),))

        def expert_ffn(k, ex, shared):
            XNT = XNT2[k % 2]
            CWM = CWM2[k % 2]
            CWAUG = CWAUG2[k % 2]
            xk = xnt_keys_of(k)
            sws = SWS
            def load_pair(p, ex=ex):
                gp = gpair["i"]
                gpair["i"] += 1
                buf = W1P[gp % 3]
                pk = ("W1P", gp % 3)
                g0 = (ex * 3 * FC + 2 * p) * 128
                op("sync", lambda e,
                   o=buf[:, :].rearrange("r (h d) -> r h d", d=D),
                   s=wgt16[g0:g0 + 256, :].rearrange("(h r) d -> r h d", r=128):
                   e.dma_start(out=o, in_=s),
                   reads=(("WG", ex, 0), ("WG", ex, 1)),
                   writes=(pk,), dma=True, dsem=f"WP{gp % 3}")
                plan.alias(("W1Gp", k, ex, p), pk)
            load_pair(0)
            load_pair(1)
            for dh in range(2):
                g2 = (ex * 3 * FC + 2 * FC) * 128
                op("sync", lambda e, dh=dh,
                   o=W2E2[dh][:, :].rearrange("r (f d) -> r f d", d=DH),
                   s=wgt16[g2:g2 + FC * 128,
                           dh * DH:(dh + 1) * DH].rearrange(
                       "(f r) d -> r f d", r=128):
                   e.dma_start(out=o, in_=s),
                   reads=(("WG2", ex),),
                   writes=tuple(("W2E", dh, j) for j in range(FC)),
                   dma=True, dsem=f"WE{dh}")
            for p in range(FC):
                wb = W1P[(gpair["i"] - FC + p - 2 + FC) % 3] if False else None
                for th in range(NTH):
                    pa = PS_A[:, th * THW:(th + 1) * THW]
                    pb = PS_B[:, th * THW:(th + 1) * THW]
                    ka = ("PSA", th % 2)
                    kb = ("PSB", th % 2)
                    wbuf = W1P[(gpair["i"] - (FC - p)) % 3] if False else None
                    def _m1a(e, p=p, th=th, pa=pa, XNT=XNT, ex=ex, k=k):
                        wb_ = w1bufs[(k, ex, p)]
                        last = None
                        for dc in range(DC):
                            last = e.matmul(
                                pa, wb_[:, dc * 128:(dc + 1) * 128],
                                XNT[:, dc * TMEGA + th * THW:
                                    dc * TMEGA + (th + 1) * THW],
                                start=(dc == 0), stop=(dc == DC - 1))
                        return last
                    op("tensor", _m1a,
                       reads=xk + (("W1Gp", k, ex, p),), writes=(ka,))
                    def _m1b(e, p=p, th=th, pb=pb, XNT=XNT, ex=ex, k=k):
                        wb_ = w1bufs[(k, ex, p)]
                        last = None
                        for dc in range(DC):
                            last = e.matmul(
                                pb, wb_[:, D + dc * 128:D + (dc + 1) * 128],
                                XNT[:, dc * TMEGA + th * THW:
                                    dc * TMEGA + (th + 1) * THW],
                                start=(dc == 0), stop=(dc == DC - 1))
                        return last
                    op("tensor", _m1b,
                       reads=xk + (("W1Gp", k, ex, p),), writes=(kb,))
                    sb = SIL[th % 2]
                    op("scalar", lambda e, sb=sb, pa=pa, p=p, ex=ex:
                       e.activation(sb[:, :], pa, AF.Silu,
                                    bias=B1S[:, ex * 2 * FC + p:
                                             ex * 2 * FC + p + 1]),
                       reads=(ka, ("B1S",)), writes=(("SIL", th % 2),))
                    op("vector", lambda e, sb=sb, pb=pb, p=p, th=th, ex=ex:
                       e.scalar_tensor_tensor(
                           sws[:, p * TMEGA + th * THW:
                               p * TMEGA + (th + 1) * THW],
                           pb, B1S[:, ex * 2 * FC + FC + p:
                                   ex * 2 * FC + FC + p + 1],
                           sb[:, :], ALU.add, ALU.mult),
                       reads=(kb, ("SIL", th % 2), ("B1S",)),
                       writes=(("SWSp", p, th),))
                if p + 2 < FC:
                    load_pair(p + 2)
            sws_keys = tuple(("SWSp", p, th)
                             for p in range(FC) for th in range(NTH))
            hm_keys = tuple(("HM", ncj, 0) for ncj in range(NC))
            for dh in range(2):
                for tc in range(NTC):
                    bk = tc % 2
                    pc = PS_C[:, bk * PCW:bk * PCW + DH]
                    kc = ("PSC", bk)
                    if shared:
                        def _ymm(e, tc=tc, dh=dh, pc=pc, bk=bk, XNT=XNT,
                                 CWAUG=CWAUG):
                            last = None
                            for ncj in range(NC):
                                last = e.matmul(
                                    pc,
                                    HMEGA[:, ncj * TMEGA + tc * 128:
                                          ncj * TMEGA + (tc + 1) * 128],
                                    WC16[:, ncj * D + dh * DH:
                                         ncj * D + (dh + 1) * DH],
                                    start=(ncj == 0), stop=False)
                            last = e.matmul(
                                pc, CWAUG[0:9, tc * 128:(tc + 1) * 128],
                                RB2A16[0:9, dh * DH:(dh + 1) * DH],
                                start=False, stop=False)
                            for q in range(NDH):
                                dc = dh * NDH + q
                                last = e.matmul(
                                    PS_C[:, bk * PCW + q * 128:
                                         bk * PCW + (q + 1) * 128],
                                    XNT[:, dc * TMEGA + tc * 128:
                                        dc * TMEGA + (tc + 1) * 128],
                                    DIAG16[:, dc * 128:(dc + 1) * 128],
                                    start=False, stop=False)
                            for p in range(FC):
                                last = e.matmul(
                                    pc,
                                    sws[:, p * TMEGA + tc * 128:
                                        p * TMEGA + (tc + 1) * 128],
                                    W2E2[dh][:, p * DH:(p + 1) * DH],
                                    start=False, stop=(p == FC - 1))
                            return last
                        op("tensor", _ymm,
                           reads=hm_keys + sws_keys + xk
                           + tuple(("WC16", j) for j in range(NC))
                           + tuple(("W2E", dh, p) for p in range(FC))
                           + tuple(("CWAUGc", t2) for t2 in range(NTC))
                           + tuple(("DIAG16", dh * NDH + q)
                                   for q in range(NDH))
                           + (("CWAUG", 8), ("RB2A", 0), ("RB2A", 8)),
                           writes=(kc,))
                        op("vector", lambda e, tc=tc, dh=dh, pc=pc:
                           e.tensor_tensor(
                               OUTA[:, tc * D + dh * DH:tc * D + (dh + 1) * DH],
                               OUTA[:, tc * D + dh * DH:tc * D + (dh + 1) * DH],
                               pc, ALU.add),
                           reads=(kc, ("OUTA", tc)), writes=(("OUTA", tc),))
                    else:
                        def _m2(e, tc=tc, dh=dh, pc=pc):
                            last = None
                            for p in range(FC):
                                last = e.matmul(
                                    pc,
                                    sws[:, p * TMEGA + tc * 128:
                                        p * TMEGA + (tc + 1) * 128],
                                    W2E2[dh][:, p * DH:(p + 1) * DH],
                                    start=(p == 0), stop=(p == FC - 1))
                            return last
                        op("tensor", _m2,
                           reads=sws_keys
                           + tuple(("W2E", dh, p) for p in range(FC)),
                           writes=(kc,))
                        op("vector", lambda e, tc=tc, dh=dh, pc=pc, ex=ex,
                           CWM=CWM:
                           e.scalar_tensor_tensor(
                               OUTA[:, tc * D + dh * DH:tc * D + (dh + 1) * DH],
                               pc, CWM[:, tc * 8 + ex - 1:tc * 8 + ex],
                               OUTA[:, tc * D + dh * DH:tc * D + (dh + 1) * DH],
                               ALU.mult, ALU.add),
                           reads=(kc, ("CWMc", k % 2, tc), ("OUTA", tc)),
                           writes=(("OUTA", tc),))

        def round_outa_init(k, eng="gpsimd"):
            t0 = k * TMEGA
            op(eng, lambda e, t0=t0: e.dma_start(
                out=OUTA[:, :].rearrange("p (tc d) -> p tc d", d=D),
                in_=x4[t0:t0 + TMEGA, :].rearrange("(tc p) d -> p tc d", p=128)),
               writes=tuple(("OUTA", tc) for tc in range(NTC)), dma=True)

        def round_store(k):
            t0 = k * TMEGA
            op("gpsimd", lambda e, t0=t0: e.dma_start(
                out=y4[t0:t0 + TMEGA, :].rearrange("(tc p) d -> p tc d", p=128),
                in_=OUTA[:, :].rearrange("p (tc d) -> p tc d", d=D)),
               reads=tuple(("OUTA", tc) for tc in range(NTC)),
               writes=(("Y4", k),), dma=True)

        w1bufs = {}
        _gp = 0
        for _k in range(NTA):
            for _exi in range(NE + 1):
                _ex = _exi + 1 if _exi < NE else 0
                for _p in range(FC):
                    w1bufs[(_k, _ex, _p)] = W1P[_gp % 3]
                    _gp += 1

        round_outa_init(0, eng="scalar")
        round_S1(0)
        for k in range(NTA):
            if k >= 1:
                for exi in range(NE):
                    expert_ffn(k - 1, exi + 1, False)
                round_cwtr(k - 1)
                carry_round(k - 1)
            if k + 1 < NTA:
                round_S1a(k + 1)
            round_gate(k)
            attn_round(k)
            if k >= 1:
                expert_ffn(k - 1, 0, True)
                round_store(k - 1)
                round_outa_init(k)
            round_pack(k)
            if k + 1 < NTA:
                round_S1T(k + 1)
        for exi in range(NE):
            expert_ffn(NTA - 1, exi + 1, False)
        round_cwtr(NTA - 1)
        carry_round(NTA - 1)
        expert_ffn(NTA - 1, 0, True)
        round_store(NTA - 1)

        plan.raw_wait("sync", DSEM["sync"], plan.cnt.get(DSEM["sync"], 0))
        plan.raw_wait("gpsimd", DSEM["gpsimd"], plan.cnt.get(DSEM["gpsimd"], 0))

        with nc.Block() as block:
            @block.sync
            def _(e):
                plan.run("sync", e, sems)

            @block.scalar
            def _(e):
                plan.run("scalar", e, sems)

            @block.vector
            def _(e):
                plan.run("vector", e, sems)

            @block.tensor
            def _(e):
                plan.run("tensor", e, sems)

            @block.gpsimd
            def _(e):
                plan.run("gpsimd", e, sems)

    return nc


def _prep_inputs(cfg, core, full):
    DC, TT, NTA, FC = cfg["DC"], cfg["TT"], cfg["NTA"], cfg["FC"]
    D = DC * 128
    TOK = TT * NTA
    NTC = cfg["TMEGA"] // 128
    b, pos = divmod(core, 4)
    NTA = cfg["NTA"]
    TTILE = TOK // NTA
    x = np.ascontiguousarray(np.concatenate(
        [full["x"][b, (k * 4 + pos) * TTILE:(k * 4 + pos + 1) * TTILE, :]
         for k in range(NTA)], axis=0))

    def chunked(v):
        return np.ascontiguousarray(v.reshape(DC, 128).T)

    w1s, w2s = [], []
    for e in range(9):
        w1 = full["w1"][e]
        w2 = full["w2"][e]
        w1s.append(np.ascontiguousarray(
            w1.reshape(DC, 128, 2 * FC, 128).transpose(2, 1, 0, 3)
            .reshape(2 * FC * 128, D)))
        w2s.append(np.ascontiguousarray(w2.reshape(FC * 128, D)))
    b1s = np.zeros((128, 9 * 2 * FC), np.float32)
    for e in range(9):
        b1s[:, e * 2 * FC:(e + 1) * 2 * FC] = \
            full["b1"][e].reshape(2 * FC, 128).T

    m8 = np.zeros((1, 8), np.float32)
    mb8 = np.zeros((1, 8), np.float32)
    for j in range(8):
        jb, jp = divmod(j, 4)
        if jb == b:
            mb8[0, j] = 1.0
            if jp < pos:
                m8[0, j] = 1.0
    m8r = np.tile(m8, (128, 1))
    mb8r = np.tile(mb8, (128, 1))

    import ml_dtypes
    return dict(
        x4=x,
        wa_t=np.ascontiguousarray(full["W_alpha_w"].T),
        wb_t=np.ascontiguousarray(full["W_B_w"].T),
        wc_t=np.ascontiguousarray(full["W_C_w"].T),
        gw_in=np.ascontiguousarray(
            full["gate_w"].reshape(DC, 128, 8).transpose(1, 0, 2)
            .reshape(128, DC * 8)),
        gb8_in=np.ascontiguousarray(np.tile(full["gate_b"][None, :], (1, NTC))),
        w1_all=np.concatenate(w1s, 0),
        w2_all=np.concatenate(w2s, 0),
        b1s_in=b1s,
        rb2_in=np.ascontiguousarray(full["rb2"]),
        wcb_in=full["W_C_b"][None, :].copy(),
        sb2_in=full["sb2"][None, :].copy(),
        alb_in=chunked(full["W_alpha_b"]),
        bbb_in=chunked(full["W_B_b"]),
        lnw_in=chunked(full["ln_w"]),
        lnb_in=chunked(full["ln_b"]),
        dpar_in=chunked(full["D_param"]),
        m8r_in=np.ascontiguousarray(m8r),
        mc8r_in=np.ascontiguousarray(1.0 - m8r),
        mb8r_in=np.ascontiguousarray(mb8r),
        mbc8r_in=np.ascontiguousarray(1.0 - mb8r),
        mi_in=np.tile(np.repeat(m8r, 2, axis=1),
                      (1, DC)).astype(ml_dtypes.bfloat16),
        mic_in=(np.tile(np.repeat(1.0 - m8r, 2, axis=1)
                        * np.tile(np.array([1.0, 0.0], np.float32), 8),
                        (1, DC))).astype(ml_dtypes.bfloat16),
        mbi_in=np.tile(np.repeat(mb8r, 2, axis=1),
                       (1, DC)).astype(ml_dtypes.bfloat16),
        mbic_in=(np.tile(np.repeat(1.0 - mb8r, 2, axis=1)
                         * np.tile(np.array([1.0, 0.0], np.float32), 8),
                         (1, DC))).astype(ml_dtypes.bfloat16),
        id16_in=np.eye(128, dtype=ml_dtypes.bfloat16),
        id32_in=np.eye(128, dtype=np.float32),
    )


def kernel(L0_attn_x, L0_attn_W_alpha_w, L0_attn_W_alpha_b, L0_attn_W_B_w,
           L0_attn_W_B_b, L0_attn_W_C_w, L0_attn_W_C_b, L0_attn_D_param,
           L0_attn_ln_w, L0_attn_ln_b, L0_ffn_gate_w, L0_ffn_gate_b,
           L0_ffn_sw1, L0_ffn_sb1, L0_ffn_sw2, L0_ffn_sb2, L0_ffn_rw1,
           L0_ffn_rb1, L0_ffn_rw2, L0_ffn_rb2):
    cfg = REAL_CFG
    D = cfg["DC"] * 128
    TOK = cfg["TT"] * cfg["NTA"]
    x = np.asarray(L0_attn_x, np.float32)
    full = dict(
        x=x,
        W_alpha_w=np.asarray(L0_attn_W_alpha_w, np.float32),
        W_alpha_b=np.asarray(L0_attn_W_alpha_b, np.float32),
        W_B_w=np.asarray(L0_attn_W_B_w, np.float32),
        W_B_b=np.asarray(L0_attn_W_B_b, np.float32),
        W_C_w=np.asarray(L0_attn_W_C_w, np.float32),
        W_C_b=np.asarray(L0_attn_W_C_b, np.float32),
        D_param=np.asarray(L0_attn_D_param, np.float32),
        ln_w=np.asarray(L0_attn_ln_w, np.float32),
        ln_b=np.asarray(L0_attn_ln_b, np.float32),
        gate_w=np.asarray(L0_ffn_gate_w, np.float32),
        gate_b=np.asarray(L0_ffn_gate_b, np.float32),
        w1=[np.asarray(L0_ffn_sw1[0], np.float32)]
           + [np.asarray(L0_ffn_rw1[e], np.float32) for e in range(8)],
        w2=[np.asarray(L0_ffn_sw2[0], np.float32)]
           + [np.asarray(L0_ffn_rw2[e], np.float32) for e in range(8)],
        b1=[np.asarray(L0_ffn_sb1[0], np.float32)]
           + [np.asarray(L0_ffn_rb1[e], np.float32) for e in range(8)],
        rb2=np.asarray(L0_ffn_rb2, np.float32),
        sb2=np.asarray(L0_ffn_sb2[0], np.float32),
    )
    nc = build_nc(cfg)
    in_maps = [_prep_inputs(cfg, c, full) for c in range(8)]
    res = run_bass_kernel_spmd(nc, in_maps, core_ids=list(range(8)))
    B, T = x.shape[0], x.shape[1]
    out = np.empty((B, T, D), np.float32)
    NTA = cfg["NTA"]
    TTILE = TOK // NTA
    for c in range(8):
        b, pos = divmod(c, 4)
        y = res.results[c]["y4"]
        for k in range(NTA):
            out[b, (k * 4 + pos) * TTILE:(k * 4 + pos + 1) * TTILE, :] = \
                y[k * TTILE:(k + 1) * TTILE, :]
    return out
